# version 03
# baseline (speedup 1.0000x reference)
import numpy as np
import concourse.bass as bass
import concourse.mybir as mybir
from concourse.bass_utils import run_bass_kernel_spmd

F32 = mybir.dt.float32
BF16 = mybir.dt.bfloat16
I32 = mybir.dt.int32
AF = mybir.ActivationFunctionType
ALU = mybir.AluOpType
AX = mybir.AxisListType

D = 2048
DI = 4096
DS = 128
NH = 64
HP = 64
CHUNK = 256
NCK = 4
LOCAL = 1024
HALO = 3
TH = LOCAL + HALO
NKT = D // 128
XT = 34
ZT = 32
IT = 64
EPS = 1e-5

DEBUG = 0
PHASES = 4
BARRIERS = 0


class Rec:
    class _Inst:
        __slots__ = ("rec",)

        def __init__(self, rec):
            self.rec = rec

        def then_inc(self, sem, n=1):
            self.rec.ops[-1][3].append((sem, n))
            return self

    def __init__(self, name):
        self.name = name
        self.ops = []

    def op(self, method, *a, **k):
        self.ops.append((method, a, k, []))
        return Rec._Inst(self)

    def __getattr__(self, method):
        if method.startswith("_"):
            raise AttributeError(method)
        return lambda *a, **k: self.op(method, *a, **k)

    def replay(self, eng):
        for idx, (method, a, k, incs) in enumerate(self.ops):
            try:
                inst = getattr(eng, method)(*a, **k)
            except Exception as e:
                raise RuntimeError(f"{self.name} op#{idx} {method} args={[getattr(x, 'shape', x) for x in a]}: {e}") from e
            for sem, n in incs:
                inst.then_inc(sem, n)


class Sem:
    def __init__(self, handle):
        self.h = handle
        self.n = 0

    def inc(self, k=1):
        self.n += k
        return self.n


def build_program():
    nc = bass.Bass("TRN2", target_bir_lowering=False, detect_race_conditions=False)

    d_u = nc.dram_tensor("u", [TH, D], F32, kind="ExternalInput").ap()
    d_wx = nc.dram_tensor("wx", [128, 35, NKT, 128], F32, kind="ExternalInput").ap()
    d_wz = nc.dram_tensor("wz", [128, ZT, NKT, 128], F32, kind="ExternalInput").ap()
    d_wo = nc.dram_tensor("wo", [128, ZT, D], F32, kind="ExternalInput").ap()
    d_wg = nc.dram_tensor("wg", [128, IT, NKT, 128], I32, kind="ExternalInput").ap()
    d_wu = nc.dram_tensor("wu", [128, IT, NKT, 128], I32, kind="ExternalInput").ap()
    d_wd = nc.dram_tensor("wd", [128, IT, D], I32, kind="ExternalInput").ap()
    d_convw = nc.dram_tensor("convw", [128, XT, 4], F32, kind="ExternalInput").ap()
    d_convb = nc.dram_tensor("convb", [128, XT], F32, kind="ExternalInput").ap()
    d_dtb = nc.dram_tensor("dtb", [64, 1], F32, kind="ExternalInput").ap()
    d_alog = nc.dram_tensor("alog", [64, 1], F32, kind="ExternalInput").ap()
    d_ddiag = nc.dram_tensor("ddiag", [128, ZT, 128], F32, kind="ExternalInput").ap()
    d_nw = nc.dram_tensor("nw", [128, ZT], F32, kind="ExternalInput").ap()
    d_gs = nc.dram_tensor("gscale", [128, IT], F32, kind="ExternalInput").ap()
    d_us = nc.dram_tensor("uscale", [128, IT], F32, kind="ExternalInput").ap()
    d_ds = nc.dram_tensor("dscale", [128, 16], F32, kind="ExternalInput").ap()
    d_identf = nc.dram_tensor("identf", [128, 128], F32, kind="ExternalInput").ap()
    d_tri = nc.dram_tensor("tri", [128, 2, CHUNK], F32, kind="ExternalInput").ap()
    d_cmb = nc.dram_tensor("cmb", [1, 128], F32, kind="ExternalInput").ap()
    d_out = nc.dram_tensor("out", [LOCAL, D], F32, kind="ExternalOutput").ap()
    d_xnsp = nc.dram_tensor("xnsp", [128, NKT, 1032], BF16).ap()
    d_stsp = nc.dram_tensor("stsp", [NCK, 128, DI], BF16).ap()
    d_rssp = nc.dram_tensor("rssp", [8, 128, D], F32).ap()
    cc_in = nc.dram_tensor("cc_in", [129, DI], BF16)
    cc_out = nc.dram_tensor("cc_out", [8 * 129, DI], BF16)

    dbg = {}

    def dbgt(name, shape, dtype=F32):
        dbg[name] = nc.dram_tensor("dbg_" + name, shape, dtype, kind="ExternalOutput").ap()

    if DEBUG >= 1:
        dbgt("stat0", [128, 16])
        dbgt("xnbf0", [128, 2, LOCAL], BF16)
        dbgt("xnT", [128, NKT, TH], BF16)
        dbgt("x_fm", [128, ZT, LOCAL], BF16)
        dbgt("BT", [128, LOCAL], BF16)
        dbgt("CT", [128, LOCAL], BF16)
        dbgt("dt", [64, LOCAL])
    if DEBUG >= 2:
        dbgt("cs", [64, NCK, CHUNK])
        dbgt("sloc", [128, DI], BF16)
        dbgt("sin", [128, DI])
        dbgt("y", [128, ZT, LOCAL], BF16)
    if DEBUG >= 3:
        dbgt("xnT2", [128, NKT, 1032], BF16)
        dbgt("zlast", [128, LOCAL], BF16)
        dbgt("sqlast", [128, LOCAL], BF16)
        dbgt("ygated", [128, ZT, LOCAL], BF16)
        dbgt("inv", [128, 8])
        dbgt("resid", [8, 128, D])
    if DEBUG >= 4:
        dbgt("xn2T", [128, NKT, LOCAL], BF16)
        dbgt("h0", [128, LOCAL], BF16)

    SBASE = 17408
    BUDGET = 229376

    def at(name, shape, dtype, off):
        off = off + SBASE
        nbytes = int(np.prod(shape[1:])) * np.dtype(mybir.dt.np(dtype)).itemsize
        assert off + nbytes <= BUDGET, (name, off, nbytes)
        return nc.alloc_sbuf_tensor_at(name, shape, dtype, offset=off).ap()

    x_fm = at("x_fm", [128, ZT, LOCAL], BF16, 0)
    resid = at("resid", [128, 8, D], F32, 0)
    BT = at("BT", [128, LOCAL], BF16, 65536)
    CT = at("CT", [128, LOCAL], BF16, 67584)
    hsb = at("hsb", [128, 16, LOCAL], BF16, 65536)
    identb = at("identb", [128, 128], BF16, 98304)
    identf = at("identf_s", [128, 128], F32, 98560)
    trib = at("trib", [128, 2, CHUNK], BF16, 99072)
    convw = at("convw_s", [128, XT, 4], F32, 100096)
    convb = at("convb_s", [128, XT], F32, 100672)
    nw = at("nw_s", [128, ZT], F32, 100832)
    gsc = at("gsc", [128, IT], F32, 100960)
    usc = at("usc", [128, IT], F32, 101216)
    dtb = at("dtb_s", [64, 1], F32, 101472)
    alog = at("alog_s", [64, 1], F32, 101504)
    acol = at("acol", [64, 1], F32, 101536)
    dtot = at("dtot", [64, 1], F32, 101568)
    ones1 = at("ones1", [1, 128], BF16, 101600)
    onescol = at("onescol", [128, 1], BF16, 101856)
    neg128 = at("neg128", [1, 128], BF16, 101888)
    small = at("small", [128, 64], F32, 102144)
    stat = at("stat", [128, 16], F32, 102400)
    wbb = at("wbb", [128, 8, 64], F32, 102464)
    cmb = at("cmb_s", [1, 128], F32, 104512)
    onesf = at("onesf", [1, 128], F32, 105024)
    csbf = at("csbf", [64, 1], BF16, 105536)
    csrow = at("csrow", [1, 1024], F32, 105600)
    colbf = at("colbf", [1, 1024], BF16, 109696)
    hsbf = at("hsbf", [1, 1024], BF16, 116736)
    dscol = at("dscol", [128, 16], F32, 112640)
    sgscr = at("sgscr", [128, LOCAL], BF16, 191104 + 16384)
    dwn = at("dwn", [128, LOCAL], BF16, 191104 + 18432)
    drow = at("drow", [1, 512], F32, 77824)
    wrow = at("wrow", [1, 512], F32, 79872)
    ddiag = at("ddiag_s", [128, ZT, 128], BF16, 69632)
    dt_t = at("dt_t", [64, LOCAL], F32, 112640)
    dA = at("dA", [64, LOCAL], F32, 116736)
    cs = at("cs_s", [64, NCK, CHUNK], F32, 120832)
    csT = at("csT", [128, NCK, 2, 64], F32, 124928)
    dtT = at("dtT", [128, NCK, 2, 64], F32, 126976)
    ddecT = at("ddecT", [128, NCK, 2, 64], F32, 129024)
    cdb = at("cdb", [128, NCK, 64], F32, 131072)
    X0 = 132608
    xnT = at("xnT", [128, NKT, 1032], BF16, X0)
    segscr = at("segscr", [128, 4, 2, CHUNK], F32, X0)
    Mt = at("Mt", [128, 4, 2, CHUNK], BF16, X0 + 8192)
    CBm = at("CBm", [128, 2, CHUNK], BF16, X0 + 12288)
    Btok = at("Btok", [128, 2, 128], BF16, 111744)
    sgbuf = at("sgbuf", [128, D], BF16, X0 + 13312)
    Eq = at("Eq", [128, 8, CHUNK], BF16, X0 + 2048)
    yoffq = at("yoffq", [128, 8, CHUNK], BF16, X0 + 6144)
    xdt = at("xdt", [128, 2, DI], BF16, X0 + 17408)
    xn2T = at("xn2T", [128, NKT, LOCAL], BF16, X0)
    Y0 = 166400
    lnscr = at("lnscr", [128, 2, D], F32, Y0)
    lnsq = at("lnsq", [128, D], F32, Y0 + 16384)
    prev = at("prev", [128, DI], F32, Y0)
    prev_bf = at("prev_bf", [128, DI], BF16, Y0 + 16384)
    ubuf = at("ubuf", [128, 2, D], F32, Y0)
    pscr = at("pscr", [128, D], F32, Y0 + 16384)
    lnsq2 = at("lnsq2", [128, D], F32, Y0 + 8192)
    Z0 = 191104
    xnbf = at("xnbf", [128, 2, D], BF16, Z0)
    convscr = at("convscr", [128, 2, 1030], F32, Z0)
    wring = at("wring", [128, 2, NKT, 128], BF16, Z0 + 8256)
    dtraw = at("dtraw", [64, TH], F32, Z0 + 16448)
    xdd2 = at("xdd2", [128, 2, DI], BF16, Z0)
    zsb = at("zsb", [128, LOCAL], BF16, Z0)
    sqsb = at("sqsb", [128, LOCAL], BF16, Z0 + 2048)
    wdo = at("wdo", [128, 2, D], BF16, Z0 + 4096)
    wgu = at("wgu", [128, 2, NKT, 128], BF16, Z0)
    wdd = at("wdd", [128, 2, D], BF16, Z0 + 8192)

    pBIG = nc.alloc_psum_tensor("pBIG", [128, 2048], F32).ap()
    pE = nc.alloc_psum_tensor("pE", [128, 512], F32).ap()
    pF = nc.alloc_psum_tensor("pF", [128, 512], F32).ap()
    pG = nc.alloc_psum_tensor("pG", [128, 512], F32).ap()
    pT = nc.alloc_psum_tensor("pT", [128, 1024], BF16).ap()

    guards = []

    def newsem(nm):
        g = nc.semaphore(nm)
        h = g.__enter__()
        guards.append(g)
        return Sem(h)

    sT = newsem("sT")
    sV = newsem("sV")
    sS = newsem("sS")
    sG = newsem("sG")
    sC = newsem("sC")
    sU = newsem("sU")
    sW = newsem("sW")
    sU0 = newsem("sU0")
    sU1 = newsem("sU1")
    sW0 = newsem("sW0")
    sW1 = newsem("sW1")
    sW2 = newsem("sW2")
    sW3 = newsem("sW3")
    sO = newsem("sO")
    sO1 = newsem("sO1")
    sP = newsem("sP")
    sQ = newsem("sQ")

    T, V, S, G, SY = Rec("tensor"), Rec("vector"), Rec("scalar"), Rec("gpsimd"), Rec("sync")

    def hard_barrier(bit):
        if not (BARRIERS & (1 << bit)):
            return
        for E in (T, V, S, G, SY):
            for sm in (sT, sV, sS, sG, sU, sW, sU0, sU1, sW0, sW1, sW2, sW3, sO, sO1, sP, sQ):
                if sm.n:
                    E.wait_ge(sm.h, sm.n)

    def chain_dma(R, sem, out, in_):
        if sem.n:
            R.wait_ge(sem.h, sem.n)
        R.dma_start(out=out, in_=in_).then_inc(sem.h, 16)
        return sem.inc(16)

    chain_dma(G, sW, identb[:, :], d_identf)
    chain_dma(G, sW, trib[:, :, :], d_tri)
    chain_dma(G, sW, ddiag[:, :, :], d_ddiag)
    chain_dma(SY, sU, identf[:, :], d_identf)
    chain_dma(SY, sU, convw[:, :, :], d_convw)
    chain_dma(SY, sU, convb[:, :], d_convb)
    chain_dma(SY, sU, dtb[:, :], d_dtb)
    chain_dma(SY, sU, alog[:, :], d_alog)
    chain_dma(SY, sU, nw[:, :], d_nw)
    chain_dma(SY, sU, gsc[:, :], d_gs)
    chain_dma(SY, sU, usc[:, :], d_us)
    chain_dma(SY, sU, cmb[:, :], d_cmb)
    CONST_U = sU.n
    CONST_W = sW.n
    V.wait_ge(sU.h, CONST_U)
    V.wait_ge(sW.h, CONST_W)
    V.memset(ones1[:, :], 1.0)
    V.memset(onescol[:, :], 1.0)
    V.memset(neg128[:, :], -128.0)
    V.memset(onesf[:, :], 1.0).then_inc(sV.h, 1)
    V_CONST = sV.inc()
    S.wait_ge(sU.h, CONST_U)
    S.activation(acol[:, :], alog[:, :], AF.Exp).then_inc(sS.h, 1)
    S_ACOL = sS.inc()

    import os
    NUT = int(os.environ.get("NUT", "9"))
    last_tr = {}
    for ut in range(NUT):
        p = ut % 2
        rows = 128 if ut < 8 else TH - 8 * 128
        sUp = sU0 if p == 0 else sU1
        if ut >= 2:
            SY.wait_ge(sS.h, sS.n)
            SY.wait_ge(sV.h, sV.n)
        SY.dma_start(out=lnscr[0:rows, p, :], in_=d_u[ut * 128: ut * 128 + rows, :]).then_inc(sUp.h, 16)
        u_ld = sUp.inc(16)
        V.wait_ge(sUp.h, u_ld)
        V.reduce_sum(stat[0:rows, 0:1], lnscr[0:rows, p, :], axis=AX.X).then_inc(sV.h, 1)
        sum_done = sV.inc()
        V.memset(stat[0:rows, 1:2], 0.0).then_inc(sV.h, 1)
        sqz = sV.inc()
        S.wait_ge(sUp.h, u_ld)
        S.wait_ge(sV.h, sqz)
        S.activation(lnsq[0:rows, :], lnscr[0:rows, p, :], AF.Square,
                     accum_out=stat[0:rows, 1:2]).then_inc(sS.h, 1)
        sq_done = sS.inc()
        V.wait_ge(sS.h, sq_done)
        V.tensor_scalar(stat[0:rows, 2:3], stat[0:rows, 0:1], 1.0 / D, None, ALU.mult)
        V.tensor_scalar(stat[0:rows, 3:4], stat[0:rows, 1:2], 1.0 / D, EPS, ALU.mult, ALU.add).then_inc(sV.h, 1)
        V.wait_ge(sV.h, sV.inc())
        V.tensor_scalar(stat[0:rows, 4:5], stat[0:rows, 2:3], stat[0:rows, 2:3], None, ALU.mult).then_inc(sV.h, 1)
        V.wait_ge(sV.h, sV.inc())
        V.tensor_sub(stat[0:rows, 5:6], stat[0:rows, 3:4], stat[0:rows, 4:5]).then_inc(sV.h, 1)
        var_done = sV.inc()
        S.wait_ge(sV.h, var_done)
        S.activation(stat[0:rows, 6:7], stat[0:rows, 5:6], AF.Ln).then_inc(sS.h, 1)
        S.wait_ge(sS.h, sS.inc())
        S.activation(stat[0:rows, 7:8], stat[0:rows, 6:7], AF.Exp, scale=-0.5).then_inc(sS.h, 1)
        rstd_done = sS.inc()
        V.wait_ge(sS.h, rstd_done)
        V.tensor_scalar(stat[0:rows, 8:9], stat[0:rows, 2:3], stat[0:rows, 7:8], -1.0,
                        ALU.mult, ALU.mult).then_inc(sV.h, 1)
        nb_done = sV.inc()
        V.wait_ge(sV.h, nb_done)
        if ut >= 2:
            V.wait_ge(sT.h, last_tr[ut - 2])
        V.tensor_scalar(xnbf[0:rows, p, :], lnscr[0:rows, p, :], stat[0:rows, 7:8],
                        stat[0:rows, 8:9], ALU.mult, ALU.add).then_inc(sV.h, 1)
        xn_done = sV.inc()
        if DEBUG >= 1 and ut == 0:
            V.tensor_copy(small[:, 0:16], stat[:, :])
        if DEBUG >= 1 and ut == 1:
            V.tensor_copy(x_fm[:, 0:2, :].rearrange("p a b -> p (a b)"), xnbf[:, 0, :])
        slotw = rows if rows % 2 == 0 else rows + 1
        for half in range(2):
            T.wait_ge(sV.h, xn_done)
            if (ut, half) != (0, 0):
                T.wait_ge(sV.h, sV.n)
            for g8 in range(8):
                kt = half * 8 + g8
                inst = T.transpose(pT[:, g8 * slotw:g8 * slotw + rows],
                                   xnbf[0:rows, p, kt * 128:(kt + 1) * 128], identb[0:rows, 0:rows])
            inst.then_inc(sT.h, 1)
            tr_done = sT.inc()
            last_tr[ut] = tr_done
            V.wait_ge(sT.h, tr_done)
            V.tensor_copy(xnT[:, half * 8:(half + 1) * 8, ut * 128: ut * 128 + rows],
                          pT[:, 0:8 * slotw].rearrange("p (g r) -> p g r", g=8)[:, :, 0:rows]).then_inc(sV.h, 1)
            sV.inc()
    LN1_V = sV.n

    if PHASES >= 1:
      T.wait_ge(sV.h, LN1_V)
      evac_done = [None, None]
      silu_done = [None, None]
      mm_done_list = []
      for mt in range(35):
          slot = mt % 2
          sWs = sW0 if slot == 0 else sW1
          if mt >= 2:
              G.wait_ge(sT.h, mm_done_list[mt - 2])
          G.dma_start(out=wring[:, slot, :, :], in_=d_wx[:, mt, :, :]).then_inc(sWs.h, 16)
          w_ld = sWs.inc(16)
          if slot == 0:
              psA, psB, psC = pBIG[:, 0:512], pBIG[:, 512:1024], pG[:, 0:3]
          else:
              psA, psB, psC = pE[:, :], pF[:, :], pG[:, 256:259]
          T.wait_ge(sWs.h, w_ld)
          if evac_done[slot] is not None:
              T.wait_ge(sV.h, evac_done[slot])
          for k in range(NKT):
              st, sp = (k == 0), (k == NKT - 1)
              T.matmul(psA, wring[:, slot, k, :], xnT[:, k, 0:512], start=st, stop=sp)
              T.matmul(psB, wring[:, slot, k, :], xnT[:, k, 512:1024], start=st, stop=sp)
          if evac_done[1 - slot] is not None:
              T.wait_ge(sV.h, evac_done[1 - slot])
          for k in range(NKT):
              inst = T.matmul(psC, wring[:, slot, k, :], xnT[:, k, 1024:1027],
                              start=(k == 0), stop=(k == NKT - 1))
          inst.then_inc(sT.h, 1)
          mm_done = sT.inc()
          mm_done_list.append(mm_done)
          V.wait_ge(sT.h, mm_done)
          if mt < 34:
              if silu_done[slot] is not None:
                  V.wait_ge(sS.h, silu_done[slot])
              V.tensor_copy(convscr[:, slot, 0:512], psA)
              V.tensor_copy(convscr[:, slot, 512:1024], psB)
              V.tensor_copy(convscr[:, slot, 1024:1027], psC).then_inc(sV.h, 1)
              evac_done[slot] = sV.inc()
              V.tensor_scalar(lnsq[:, 0:1024], convscr[:, slot, 0:1024], convw[:, mt, 0:1], None, ALU.mult)
              for kk in range(1, 3):
                  V.scalar_tensor_tensor(lnsq[:, 0:1024], convscr[:, slot, kk:kk + 1024],
                                         convw[:, mt, kk:kk + 1], lnsq[:, 0:1024], ALU.mult, ALU.add)
              V.scalar_tensor_tensor(lnsq[:, 0:1024], convscr[:, slot, 3:3 + 1024],
                                     convw[:, mt, 3:4], lnsq[:, 0:1024], ALU.mult, ALU.add).then_inc(sV.h, 1)
              conv_done = sV.inc()
              S.wait_ge(sV.h, conv_done)
              dest = x_fm[:, mt, :] if mt < 32 else (BT[:, :] if mt == 32 else CT[:, :])
              S.activation(dest, lnsq[:, 0:1024], AF.Silu, bias=convb[:, mt:mt + 1]).then_inc(sS.h, 1)
              silu_done[slot] = sS.inc()
          else:
              V.tensor_copy(dtraw[:, 0:512], psA[0:64, :])
              V.tensor_copy(dtraw[:, 512:1024], psB[0:64, :])
              V.tensor_copy(dtraw[:, 1024:1027], psC[0:64, :]).then_inc(sV.h, 1)
              evac_done[slot] = sV.inc()
              S.wait_ge(sV.h, evac_done[slot])
              S.activation(lnsq[0:64, 0:1024], dtraw[:, HALO:TH], AF.Relu, bias=dtb[:, 0:1])
              S.activation(lnsq[0:64, 1024:2048], dtraw[:, HALO:TH], AF.Abs, bias=dtb[:, 0:1])
              S.activation(lnsq[0:64, 1024:2048], lnsq[0:64, 1024:2048], AF.Exp, scale=-1.0)
              S.activation(lnsq[0:64, 1024:2048], lnsq[0:64, 1024:2048], AF.Ln, bias=1.0).then_inc(sS.h, 1)
              sp_done = sS.inc()
              V.wait_ge(sS.h, sp_done)
              V.tensor_add(dt_t[:, :], lnsq[0:64, 0:1024], lnsq[0:64, 1024:2048]).then_inc(sV.h, 1)
              sV.inc()
              V.wait_ge(sS.h, S_ACOL)
              V.tensor_scalar(dA[:, :], dt_t[:, :], acol[:, 0:1], -1.0, ALU.mult, ALU.mult).then_inc(sV.h, 1)
              sV.inc()
      P1_V = sV.n
      P1_S = sS.n
      P1_T = sT.n

    if DEBUG >= 1:
        G.wait_ge(sV.h, sV.n)
        G.wait_ge(sS.h, sS.n)
        chain_dma(G, sO1, dbg["stat0"], small[:, 0:16])
        chain_dma(G, sO1, dbg["xnbf0"], x_fm[:, 0:2, :])
        chain_dma(G, sO1, dbg["xnT"], xnT[:, :, 0:TH])
        if PHASES >= 1:
            chain_dma(G, sO1, dbg["x_fm"], x_fm[:, :, :])
            chain_dma(G, sO1, dbg["BT"], BT[:, :])
            chain_dma(G, sO1, dbg["CT"], CT[:, :])
            chain_dma(G, sO1, dbg["dt"], dt_t[:, :])


    if PHASES >= 2:
        CSL = CHUNK - 1

        def vedge():
            V.ops[-1][3].append((sV.h, 1))
            V.wait_ge(sV.h, sV.inc())

        G.wait_ge(sV.h, sV.n)
        G.wait_ge(sS.h, sS.n)
        chain_dma(G, sP, d_xnsp, xnT[:, :, :])
        xnsp_done = sP.n
        V.wait_ge(sP.h, xnsp_done)
        if DEBUG:
            V.wait_ge(sO1.h, sO1.n)
        dA4 = dA[:, :].rearrange("p (c l) -> p c l", c=NCK)
        cs4 = cs[:, :, :]
        sg4 = segscr[0:64, 0:2, :, :].rearrange("p a b l -> p (a b) l")
        src4 = dA4
        dst4 = cs4
        k = 1
        step = 0
        while k < CHUNK:
            V.tensor_copy(dst4[:, :, 0:k], src4[:, :, 0:k])
            V.tensor_add(dst4[:, :, k:CHUNK], src4[:, :, k:CHUNK], src4[:, :, 0:CHUNK - k])
            vedge()
            src4, dst4 = dst4, (sg4 if step % 2 == 0 else cs4)
            k *= 2
            step += 1
        V.tensor_copy(cs4[:, :, :], sg4[:, :, :])
        vedge()
        CS_DONE = sV.n
        V.reduce_sum(dtot[:, 0:1], dA[:, :], axis=AX.X).then_inc(sV.h, 1)
        DTOT_V = sV.inc()
        S.wait_ge(sV.h, DTOT_V)
        S.activation(dtot[:, 0:1], dtot[:, 0:1], AF.Exp).then_inc(sS.h, 1)
        DTOT_S = sS.inc()
        V.memset(prev[:, :], 0.0).then_inc(sV.h, 1)
        PREV0 = sV.inc()

        st_spill = []
        msub_done = {}
        exp_done = {}
        gm_done = {}
        ymm_done = {}
        ypair_mm = {}
        for c in range(NCK):
            t0 = c * CHUNK
            csc = cs[:, c, :]
            T.wait_ge(sV.h, CS_DONE)
            if c > 0:
                T.wait_ge(sV.h, sV.n)
            for sh in range(2):
                T.transpose(pG[:, 0:64], csc[:, sh * 128:(sh + 1) * 128], identf[0:64, 0:64])
                T.ops[-1][3].append((sT.h, 1))
                trd = sT.inc()
                V.wait_ge(sT.h, trd)
                V.tensor_copy(csT[:, c, sh, :], pG[:, 0:64]).then_inc(sV.h, 1)
                sV.inc()
                T.wait_ge(sV.h, sV.n)
                T.transpose(pG[:, 0:64], dt_t[:, t0 + sh * 128: t0 + (sh + 1) * 128],
                            identf[0:64, 0:64])
                T.ops[-1][3].append((sT.h, 1))
                trd = sT.inc()
                V.wait_ge(sT.h, trd)
                V.tensor_copy(dtT[:, c, sh, :], pG[:, 0:64]).then_inc(sV.h, 1)
                sV.inc()
                T.wait_ge(sV.h, sV.n)
            e1s = segscr[0:64, 1, 0, :]
            dds = segscr[0:64, 1, 1, :]
            S.wait_ge(sV.h, CS_DONE)
            S.activation(e1s, csc[:, :], AF.Exp, scale=-1.0, bias=csc[:, CSL:CSL + 1]).then_inc(sS.h, 1)
            e1_done = sS.inc()
            V.wait_ge(sS.h, e1_done)
            V.tensor_mul(dds, dt_t[:, t0:t0 + CHUNK], e1s)
            vedge()
            for sh in range(2):
                T.wait_ge(sV.h, sV.n)
                T.transpose(pG[:, 64:128], dds[:, sh * 128:(sh + 1) * 128], identf[0:64, 0:64])
                T.ops[-1][3].append((sT.h, 1))
                trd = sT.inc()
                V.wait_ge(sT.h, trd)
                V.tensor_copy(ddecT[:, c, sh, :], pG[:, 64:128]).then_inc(sV.h, 1)
                sV.inc()
            T.wait_ge(sV.h, sV.n)
            T.transpose(pG[0:1, 128:192], csc[:, CSL:CSL + 1], identf[0:64, 0:64])
            T.ops[-1][3].append((sT.h, 1))
            trd = sT.inc()
            S.wait_ge(sT.h, trd)
            S.activation(small[0:1, 0:64], pG[0:1, 128:192], AF.Exp).then_inc(sS.h, 1)
            cdrow_done = sS.inc()
            T.wait_ge(sS.h, cdrow_done)
            T.matmul(pG[:, 192:256], onesf[:, :], small[0:1, 0:64], start=True, stop=True)
            T.ops[-1][3].append((sT.h, 1))
            trd = sT.inc()
            V.wait_ge(sT.h, trd)
            V.tensor_copy(cdb[:, c, :], pG[:, 192:256]).then_inc(sV.h, 1)
            sV.inc()

            T.wait_ge(sS.h, sS.n)
            for sh in range(2):
                T.matmul(pF[:, sh * 256:(sh + 1) * 256], BT[:, t0 + sh * 128: t0 + (sh + 1) * 128],
                         CT[:, t0:t0 + CHUNK], start=True, stop=True)
            T.ops[-1][3].append((sT.h, 1))
            cb_done = sT.inc()
            V.wait_ge(sT.h, cb_done)
            V.tensor_mul(CBm[:, :, :].rearrange("p a l -> p (a l)"), pF[:, :],
                         trib[:, :, :].rearrange("p a l -> p (a l)")).then_inc(sV.h, 1)
            cbm_done = sV.inc()

            for sh in range(2):
                for gq in range(4):
                    tiles = range(gq * 8, (gq + 1) * 8)
                    if not (c == 0 and sh == 0 and gq == 0):
                        T.wait_ge(sV.h, sV.n)
                    for j, t in enumerate(tiles):
                        inst = T.transpose(pT[:, j * 128:(j + 1) * 128],
                                           x_fm[:, t, t0 + sh * 128: t0 + (sh + 1) * 128],
                                           identb[:, :])
                    inst.then_inc(sT.h, 1)
                    trd = sT.inc()
                    V.wait_ge(sT.h, trd)
                    dtbc = dtT[:, c, sh, 16 * gq:16 * gq + 16].unsqueeze(2).broadcast_to([128, 16, 64])
                    ddbc = ddecT[:, c, sh, 16 * gq:16 * gq + 16].unsqueeze(2).broadcast_to([128, 16, 64])
                    pTv = pT[:, 0:1024].rearrange("p (a b) -> p a b", a=16)
                    V.tensor_tensor(xdt[:, sh, 1024 * gq:1024 * (gq + 1)].rearrange("p (a b) -> p a b", a=16),
                                    pTv, dtbc, ALU.mult)
                    V.tensor_tensor(xdd2[:, sh, 1024 * gq:1024 * (gq + 1)].rearrange("p (a b) -> p a b", a=16),
                                    pTv, ddbc, ALU.mult).then_inc(sV.h, 1)
                    sV.inc()
                T.wait_ge(sV.h, sV.n)
                T.transpose(pT[:, 0:128], BT[:, t0 + sh * 128: t0 + (sh + 1) * 128], identb[:, :])
                T.ops[-1][3].append((sT.h, 1))
                trd = sT.inc()
                V.wait_ge(sT.h, trd)
                V.tensor_copy(Btok[:, sh, :], pT[:, 0:128]).then_inc(sV.h, 1)
                btok_done = sV.inc()
            for j in range(8):
                ps = pE if j % 2 == 0 else pF
                T.wait_ge(sV.h, btok_done)
                if c + j > 0:
                    T.wait_ge(sV.h, sV.n)
                for sh in range(2):
                    T.matmul(ps[:, :], Btok[:, sh, :], xdd2[:, sh, j * 512:(j + 1) * 512],
                             start=(sh == 0), stop=(sh == 1))
                T.ops[-1][3].append((sT.h, 1))
                st_done = sT.inc()
                V.wait_ge(sT.h, st_done)
                if j % 4 == 0 and (c > 0 or j > 0):
                    V.wait_ge(sP.h, sP.n)
                V.tensor_copy(sgbuf[:, (j % 4) * 512:(j % 4 + 1) * 512], ps[:, :])
                vedge()
                if c == 0:
                    V.tensor_copy(prev[:, j * 512:(j + 1) * 512], ps[:, :]).then_inc(sV.h, 1)
                    sV.inc()
                else:
                    cdbc = cdb[:, c, 8 * j:8 * j + 8].unsqueeze(2).broadcast_to([128, 8, 64])
                    prv = prev[:, j * 512:(j + 1) * 512].rearrange("p (a b) -> p a b", a=8)
                    V.tensor_tensor(prv, prv, cdbc, ALU.mult)
                    vedge()
                    V.tensor_tensor(prev[:, j * 512:(j + 1) * 512],
                                    prev[:, j * 512:(j + 1) * 512], ps[:, :], ALU.add).then_inc(sV.h, 1)
                    sV.inc()
                if j % 4 == 3:
                    G.wait_ge(sV.h, sV.n)
                    chain_dma(G, sP, d_stsp[c].rearrange("p (a b) -> p a b", a=2)[:, j // 4, :],
                              sgbuf[:, :])
            st_spill.append(sP.n)

            if c == 0:
                T.wait_ge(sV.h, sV.n)
            for h in range(NH):
                hg = c * NH + h
                q = h % 4
                pr = h // 2
                prs = pr % 8
                pbc = (pG if q < 2 else pE)[:, 256 * (q % 2): 256 * (q % 2 + 1)]
                if hg >= 4:
                    T.wait_ge(sV.h, msub_done[hg - 4])
                else:
                    T.wait_ge(sV.h, CS_DONE)
                oh = identf[0:64, h:h + 1].broadcast_to([64, 128])
                T.matmul(pbc, oh, csc[:, :], start=True, stop=True)
                T.ops[-1][3].append((sT.h, 1))
                bc_done = sT.inc()
                V.wait_ge(sT.h, bc_done)
                if hg >= 4:
                    V.wait_ge(sS.h, exp_done[hg - 4])
                for sh in range(2):
                    V.scalar_tensor_tensor(segscr[:, q, sh, :], pbc, csT[:, c, sh, h:h + 1],
                                           trib[:, sh, :], ALU.subtract, ALU.mult)
                V.ops[-1][3].append((sV.h, 1))
                msub_done[hg] = sV.inc()
                S.wait_ge(sV.h, msub_done[hg])
                if hg >= 4:
                    S.wait_ge(sG.h, gm_done[hg - 4])
                    S.wait_ge(sT.h, ymm_done[hg - 4])
                S.activation(Mt[:, q, :, :].rearrange("p a l -> p (a l)"),
                             segscr[:, q, :, :].rearrange("p a l -> p (a l)"),
                             AF.Exp).then_inc(sS.h, 1)
                exp_done[hg] = sS.inc()
                G.wait_ge(sS.h, exp_done[hg])
                G.wait_ge(sV.h, cbm_done)
                G.tensor_tensor(Mt[:, q, :, :].rearrange("p a l -> p (a l)"),
                                Mt[:, q, :, :].rearrange("p a l -> p (a l)"),
                                CBm[:, :, :].rearrange("p a l -> p (a l)"), ALU.mult).then_inc(sG.h, 1)
                gm_done[hg] = sG.inc()
                T.wait_ge(sG.h, gm_done[hg])
                pY = pBIG[:, 256 * prs: 256 * (prs + 1)]
                hq = h % 2
                if hq == 0:
                    if prs in ypair_mm:
                        T.wait_ge(sV.h, ypair_mm[prs])
                    T.matmul(pY, ddiag[:, pr, :], x_fm[:, pr, t0:t0 + CHUNK],
                             start=True, stop=True)
                for sh in range(2):
                    T.matmul(pY[64 * hq:64 * (hq + 1), :], xdt[:, sh, 64 * h: 64 * (h + 1)],
                             Mt[:, q, sh, :], start=False, stop=True,
                             tile_position=(0, 64 * hq))
                T.ops[-1][3].append((sT.h, 1))
                ymm_done[hg] = sT.inc()
                if hq == 1 and pr % 2 == 1:
                    V.wait_ge(sT.h, ymm_done[hg])
                    pslot0 = (pr - 1) % 8
                    V.tensor_copy(
                        x_fm[:, pr - 1: pr + 1, t0:t0 + CHUNK],
                        pBIG[:, 256 * pslot0: 256 * (pslot0 + 2)].rearrange("p (a l) -> p a l", a=2),
                    ).then_inc(sV.h, 1)
                    ypair_mm[pslot0] = sV.inc()
                    ypair_mm[(pr) % 8] = ypair_mm[pslot0]
        P2A_V = sV.n

        V.wait_ge(sV.h, P2A_V)
        V.wait_ge(sP.h, st_spill[-1])
        for _hf in range(2):
            if _hf:
                V.wait_ge(sQ.h, sQ.n)
            V.tensor_copy(sgbuf[:, :], prev[:, 2048 * _hf: 2048 * (_hf + 1)]).then_inc(sV.h, 1)
            _sl = sV.inc()
            G.wait_ge(sV.h, _sl)
            chain_dma(G, sQ, cc_in.ap()[0:128, 2048 * _hf: 2048 * (_hf + 1)], sgbuf[:, :])
        sloc_v = sV.n
        T.wait_ge(sS.h, DTOT_S)
        T.transpose(pG[0:1, 448:512], dtot[:, 0:1], identf[0:64, 0:64])
        T.ops[-1][3].append((sT.h, 1))
        dtr_done = sT.inc()
        V.wait_ge(sT.h, dtr_done)
        V.tensor_copy(csbf[0:1, 0:1].rearrange("p a -> p a") if False else Btok[0:1, 0, 0:64],
                      pG[0:1, 448:512]).then_inc(sV.h, 1)
        dtrow_v = sV.inc()
        G.wait_ge(sV.h, dtrow_v)
        chain_dma(G, sQ, cc_in.ap()[128:129, 0:64], Btok[0:1, 0, 0:64])
        G.collective_compute(
            "AllGather", ALU.bypass,
            replica_groups=[list(range(8))],
            ins=[cc_in.ap().opt()],
            outs=[cc_out.ap().opt()],
        ).then_inc(sC.h, 1)
        sC.inc()

        if DEBUG >= 2:
            G.wait_ge(sV.h, sV.n)
            G.wait_ge(sQ.h, sQ.n)
            chain_dma(G, sO1, dbg["cs"], cs[:, :, :])
            chain_dma(G, sO1, dbg["sloc"], cc_in.ap()[0:128, :])

    if PHASES >= 2:
        G.wait_ge(sC.h, 1)
        chain_dma(G, sQ, drow[0:1, :], cc_out.ap()[128::129, 0:64])
        drow_ld = sQ.n
        V.wait_ge(sQ.h, drow_ld)
        for r in range(8):
            wsl = wrow[0:1, 64 * r: 64 * (r + 1)]
            for m in range(8):
                dsl = drow[0:1, 64 * m: 64 * (m + 1)]
                fa = cmb[0:1, 8 * r + m: 8 * r + m + 1]
                fb = cmb[0:1, 64 + 8 * r + m: 64 + 8 * r + m + 1]
                if m == 0:
                    V.tensor_scalar(wsl, dsl, fa, fb, ALU.mult, ALU.add)
                    vedge()
                else:
                    V.tensor_scalar(small[0:1, 0:64], dsl, fa, fb, ALU.mult, ALU.add)
                    vedge()
                    V.tensor_mul(wsl, wsl, small[0:1, 0:64])
                    vedge()
        WROW_V = sV.n
        T.wait_ge(sV.h, WROW_V)
        T.matmul(pG[:, 0:512], onesf[:, :], wrow[0:1, :], start=True, stop=True)
        T.ops[-1][3].append((sT.h, 1))
        wb_done = sT.inc()
        V.wait_ge(sT.h, wb_done)
        V.tensor_copy(wbb[:, :, :], pG[:, 0:512].rearrange("p (a b) -> p a b", a=8)).then_inc(sV.h, 1)
        wbb_v = sV.inc()
        sgb = xdd2[:, 0, :]
        cscr = segscr[:, 3, :, :].rearrange("p a l -> p (a l)")
        for r in range(8):
            G.wait_ge(sV.h, sV.n)
            chain_dma(G, sQ, sgb, cc_out.ap()[129 * r: 129 * r + 128, :])
            sg_ld = sQ.n
            V.wait_ge(sQ.h, sg_ld)
            V.wait_ge(sV.h, wbb_v)
            for j in range(8):
                wbc = wbb[:, r, 8 * j:8 * j + 8].unsqueeze(2).broadcast_to([128, 8, 64])
                V.tensor_tensor(cscr.rearrange("p (a b) -> p a b", a=8),
                                sgb[:, 512 * j: 512 * (j + 1)].rearrange("p (a b) -> p a b", a=8),
                                wbc, ALU.mult)
                vedge()
                if r == 0:
                    V.tensor_copy(prev[:, 512 * j:512 * (j + 1)], cscr)
                    vedge()
                else:
                    V.tensor_add(prev[:, 512 * j:512 * (j + 1)],
                                 prev[:, 512 * j:512 * (j + 1)], cscr)
                    vedge()
        SIN_V = sV.n
        if DEBUG >= 2:
            G.wait_ge(sV.h, SIN_V)
            chain_dma(G, sO1, dbg["sin"], prev[:, :])
            V.wait_ge(sO1.h, sO1.n)

        eb_done = {}
        ypair_e = {}
        qcomb = {}
        for c in range(NCK):
            t0 = c * CHUNK
            csc = cs[:, c, :]
            V.wait_ge(sV.h, sV.n)
            V.wait_ge(sT.h, sT.n)
            V.tensor_copy(prev_bf[:, :], prev[:, :]).then_inc(sV.h, 1)
            pbf_v = sV.inc()
            G.wait_ge(sV.h, pbf_v)
            chain_dma(G, sQ, sgb, d_stsp[c])
            stld = sQ.n
            for qq in range(4):
                for hh in range(16):
                    h = 16 * qq + hh
                    hg = c * NH + h
                    q = h % 2
                    pr = h // 2
                    prs = pr % 8
                    pbc = pG[:, 256 * q: 256 * (q + 1)]
                    if hg >= 2:
                        T.wait_ge(sS.h, eb_done[hg - 2])
                    else:
                        T.wait_ge(sV.h, SIN_V)
                    T.wait_ge(sV.h, pbf_v)
                    oh = identf[0:64, h:h + 1].broadcast_to([64, 128])
                    T.matmul(pbc, oh, csc[:, :], start=True, stop=True)
                    T.ops[-1][3].append((sT.h, 1))
                    bc_done = sT.inc()
                    S.wait_ge(sT.h, bc_done)
                    qg = c * 4 + qq
                    if qg > 0 and hh == 0:
                        S.wait_ge(sV.h, qcomb[qg - 1])
                    S.activation(Eq[64 * q:64 * (q + 1), pr % 8, :], pbc[64 * q:64 * (q + 1), :],
                                 AF.Exp).then_inc(sS.h, 1)
                    eb_done[hg] = sS.inc()
                    pY = pBIG[:, 256 * prs: 256 * (prs + 1)]
                    if q == 0 and (prs in ypair_e):
                        T.wait_ge(sV.h, ypair_e[prs])
                    T.matmul(pY[64 * q:64 * (q + 1), :], prev_bf[:, 64 * h:64 * (h + 1)],
                             CT[:, t0:t0 + CHUNK], start=True, stop=True,
                             tile_position=(0, 64 * q))
                    T.ops[-1][3].append((sT.h, 1))
                    yo_done = sT.inc()
                    if q == 1 and pr % 2 == 1:
                        V.wait_ge(sT.h, yo_done)
                        pslot0 = (pr - 1) % 8
                        vsl = (pr - 1) % 8
                        V.tensor_copy(
                            yoffq[:, vsl:vsl + 2, :],
                            pBIG[:, 256 * pslot0: 256 * (pslot0 + 2)].rearrange("p (a l) -> p a l", a=2),
                        ).then_inc(sV.h, 1)
                        ypair_e[pslot0] = sV.inc()
                        ypair_e[(pslot0 + 1) % 8] = ypair_e[pslot0]
                V.wait_ge(sS.h, eb_done[c * NH + 16 * qq + 15])
                V.wait_ge(sV.h, sV.n)
                V.tensor_mul(yoffq[:, :, :].rearrange("p a l -> p (a l)"),
                             yoffq[:, :, :].rearrange("p a l -> p (a l)"),
                             Eq[:, :, :].rearrange("p a l -> p (a l)"))
                vedge()
                xsl = x_fm[:, 8 * qq:8 * (qq + 1), t0:t0 + CHUNK]
                V.tensor_add(xsl, xsl,
                             yoffq[:, :, :]).then_inc(sV.h, 1)
                qcomb[qg] = sV.inc()
            if c < NCK - 1:
                V.wait_ge(sQ.h, stld)
                for j in range(8):
                    cdbc = cdb[:, c, 8 * j:8 * j + 8].unsqueeze(2).broadcast_to([128, 8, 64])
                    prv = prev[:, j * 512:(j + 1) * 512].rearrange("p (a b) -> p a b", a=8)
                    V.tensor_tensor(prv, prv, cdbc, ALU.mult)
                    vedge()
                    V.tensor_add(prev[:, j * 512:(j + 1) * 512], prev[:, j * 512:(j + 1) * 512],
                                 sgb[:, 512 * j:512 * (j + 1)])
                    vedge()
        P2B_V = sV.n
        if DEBUG >= 2:
            G.wait_ge(sV.h, P2B_V)
            chain_dma(G, sO1, dbg["y"], x_fm[:, :, :])


    if PHASES >= 3:
        hard_barrier(0)
        if DEBUG:
            V.wait_ge(sO1.h, sO1.n)
        G.wait_ge(sV.h, P2B_V)
        chain_dma(G, sP, xnT[:, :, :], d_xnsp)
        xn_rld = sP.n
        if DEBUG >= 3:
            G.wait_ge(sP.h, xn_rld)
            chain_dma(G, sO1, dbg["xnT2"], xnT[:, :, :])
        T.wait_ge(sP.h, xn_rld)
        zmm_done = {}
        zevac_done = {}
        for mt in range(ZT):
            slot = mt % 2
            sWs = sW0 if slot == 0 else sW1
            if mt >= 2:
                G.wait_ge(sT.h, zmm_done[mt - 2])
            else:
                G.wait_ge(sV.h, P2B_V)
            G.dma_start(out=wring[:, slot, :, :], in_=d_wz[:, mt, :, :]).then_inc(sWs.h, 16)
            w_ld = sWs.inc(16)
            ps = pBIG[:, 1024 * slot: 1024 * (slot + 1)]
            T.wait_ge(sWs.h, w_ld)
            if mt >= 2:
                T.wait_ge(sS.h, zevac_done[mt - 2])
            for k in range(NKT):
                st, sp = (k == 0), (k == NKT - 1)
                T.matmul(ps[:, 0:512], wring[:, slot, k, :], xnT[:, k, HALO:HALO + 512], start=st, stop=sp)
                T.matmul(ps[:, 512:1024], wring[:, slot, k, :], xnT[:, k, HALO + 512:HALO + 1024], start=st, stop=sp)
            T.ops[-1][3].append((sT.h, 1))
            zmm_done[mt] = sT.inc()
            S.wait_ge(sT.h, zmm_done[mt])
            if mt >= 1:
                S.wait_ge(sV.h, sV.n)
            S.activation(zsb[:, :], ps[:, :], AF.Silu).then_inc(sS.h, 1)
            zevac_done[mt] = sS.inc()
            V.wait_ge(sS.h, zevac_done[mt])
            V.tensor_mul(x_fm[:, mt, :], x_fm[:, mt, :], zsb[:, :]).then_inc(sV.h, 1)
            g0_done = sV.inc()
            S.wait_ge(sV.h, g0_done)
            if mt >= 1:
                S.wait_ge(sT.h, sT.n)
            S.activation(sqsb[:, :], x_fm[:, mt, :], AF.Square).then_inc(sS.h, 1)
            sq_done = sS.inc()
            V.wait_ge(sS.h, sq_done)
            V.tensor_scalar(x_fm[:, mt, :], x_fm[:, mt, :], nw[:, mt:mt + 1], None,
                            ALU.mult).then_inc(sV.h, 1)
            sV.inc()
            T.wait_ge(sS.h, sq_done)
            T.matmul(pE[0:1, :], onescol[:, 0:1], sqsb[:, 0:512], start=(mt == 0), stop=(mt == ZT - 1))
            T.matmul(pF[0:1, :], onescol[:, 0:1], sqsb[:, 512:1024], start=(mt == 0), stop=(mt == ZT - 1))
            T.ops[-1][3].append((sT.h, 1))
            sT.inc()
        ZP_T = sT.n
        ZP_V = sV.n
        if DEBUG >= 3:
            G.wait_ge(sS.h, sS.n)
            G.wait_ge(sV.h, ZP_V)
            chain_dma(G, sO1, dbg["zlast"], zsb[:, :])
            chain_dma(G, sO1, dbg["sqlast"], sqsb[:, :])
            V.wait_ge(sO1.h, sO1.n)
        V.wait_ge(sT.h, ZP_T)
        V.tensor_copy(csrow[0:1, 0:512], pE[0:1, :])
        V.tensor_copy(csrow[0:1, 512:1024], pF[0:1, :])
        vedge()
        V.tensor_scalar(csrow[0:1, :], csrow[0:1, :], 1.0 / DI, EPS, ALU.mult, ALU.add)
        vedge()
        S.wait_ge(sV.h, sV.n)
        S.activation(csrow[0:1, :], csrow[0:1, :], AF.Ln).then_inc(sS.h, 1)
        S.wait_ge(sS.h, sS.inc())
        S.activation(csrow[0:1, :], csrow[0:1, :], AF.Exp, scale=-0.5).then_inc(sS.h, 1)
        inv_done = sS.inc()
        T.wait_ge(sS.h, inv_done)
        for tt in range(8):
            T.transpose(pG[:, 8 * tt: 8 * tt + 8][:, 0:1], csrow[0:1, 128 * tt:128 * (tt + 1)],
                        identf[0:1, 0:1])
        T.ops[-1][3].append((sT.h, 1))
        invt_done = sT.inc()
        V.wait_ge(sT.h, invt_done)
        V.tensor_copy(small[:, 0:8], pG[:, 0:64].rearrange("p (a b) -> p a b", a=8)[:, :, 0]).then_inc(sV.h, 1)
        invcol_v = sV.inc()
        hard_barrier(1)
        T.wait_ge(sV.h, ZP_V)
        rs_spill = []
        for tt in range(8):
            slot = tt % 2
            sWs = sW2 if slot == 0 else sW3
            sUp = sU0 if slot == 0 else sU1
            if tt >= 2:
                SY.wait_ge(sV.h, sV.n)
            else:
                SY.wait_ge(sV.h, P2B_V)
            SY.dma_start(out=ubuf[:, slot, :],
                         in_=d_u[HALO + 128 * tt: HALO + 128 * (tt + 1), :]).then_inc(sUp.h, 16)
            u_ld = sUp.inc(16)
            op_mm = {}
            for k in range(ZT):
                ks = k % 2
                sWk = sW2 if ks == 0 else sW3
                if (tt, k) in ((0, 0), (0, 1)):
                    G.wait_ge(sT.h, zmm_done[ZT - 1])
                else:
                    G.wait_ge(sT.h, op_mm.get(k - 2, sT.n if k < 2 else 0) if k >= 2 else sT.n)
                G.dma_start(out=wdo[:, ks, :], in_=d_wo[:, k, :]).then_inc(sWk.h, 16)
                wo_ld = sWk.inc(16)
                T.wait_ge(sWk.h, wo_ld)
                st, sp = (k == 0), (k == ZT - 1)
                for nn in range(4):
                    T.matmul(pBIG[:, 512 * nn: 512 * (nn + 1)],
                             x_fm[:, k, 128 * tt: 128 * (tt + 1)],
                             wdo[:, ks, 512 * nn: 512 * (nn + 1)], start=st, stop=sp)
                T.ops[-1][3].append((sT.h, 1))
                op_mm[k] = sT.inc()
            op_done = op_mm[ZT - 1]
            V.wait_ge(sT.h, op_done)
            V.wait_ge(sUp.h, u_ld)
            V.wait_ge(sV.h, invcol_v)
            if tt >= 1:
                V.wait_ge(sP.h, rs_spill[-1])
            V.scalar_tensor_tensor(pscr[:, :], pBIG[:, :], small[:, tt:tt + 1],
                                   ubuf[:, slot, :], ALU.mult, ALU.add).then_inc(sV.h, 1)
            rs_v = sV.inc()
            G.wait_ge(sV.h, rs_v)
            chain_dma(G, sP, d_rssp[tt], pscr[:, :])
            rs_spill.append(sP.n)
        P3_V = sV.n
        if DEBUG >= 3:
            G.wait_ge(sV.h, sV.n)
            chain_dma(G, sO1, dbg["ygated"], x_fm[:, :, :])
            chain_dma(G, sO1, dbg["inv"], small[:, 0:8])

    if PHASES >= 4:
        hard_barrier(2)
        T_P3 = sT.n
        if DEBUG:
            G.wait_ge(sO1.h, sO1.n)
        G.wait_ge(sP.h, rs_spill[-1])
        for tt in range(8):
            G.wait_ge(sT.h, T_P3)
            chain_dma(G, sQ, resid[:, tt, :], d_rssp[tt])
        rs_rld = sQ.n
        if DEBUG >= 3:
            G.wait_ge(sQ.h, rs_rld)
            for _tt in range(8):
                chain_dma(G, sO1, dbg["resid"][_tt], resid[:, _tt, :])
            V.wait_ge(sO1.h, sO1.n)
        SY.wait_ge(sQ.h, rs_rld)
        chain_dma(SY, sU, dscol[:, :], d_ds)
        ds_ld = sU.n
        xn2bf = lnsq2.bitcast(BF16)
        for tt in range(8):
            V.wait_ge(sQ.h, rs_rld)
            V.memset(stat[:, 1:2], 0.0).then_inc(sV.h, 1)
            sqz = sV.inc()
            V.reduce_sum(stat[:, 0:1], resid[:, tt, :], axis=AX.X).then_inc(sV.h, 1)
            sum_done = sV.inc()
            S.wait_ge(sV.h, sqz)
            if tt >= 1:
                S.wait_ge(sT.h, sT.n)
            S.activation(pscr[:, :], resid[:, tt, :], AF.Square,
                         accum_out=stat[:, 1:2]).then_inc(sS.h, 1)
            sq_done = sS.inc()
            V.wait_ge(sS.h, sq_done)
            V.tensor_scalar(stat[:, 2:3], stat[:, 0:1], 1.0 / D, None, ALU.mult)
            V.tensor_scalar(stat[:, 3:4], stat[:, 1:2], 1.0 / D, EPS, ALU.mult, ALU.add).then_inc(sV.h, 1)
            V.wait_ge(sV.h, sV.inc())
            V.tensor_scalar(stat[:, 4:5], stat[:, 2:3], stat[:, 2:3], None, ALU.mult).then_inc(sV.h, 1)
            V.wait_ge(sV.h, sV.inc())
            V.tensor_sub(stat[:, 5:6], stat[:, 3:4], stat[:, 4:5]).then_inc(sV.h, 1)
            var_done = sV.inc()
            S.wait_ge(sV.h, var_done)
            S.activation(stat[:, 6:7], stat[:, 5:6], AF.Ln).then_inc(sS.h, 1)
            S.wait_ge(sS.h, sS.inc())
            S.activation(stat[:, 7:8], stat[:, 6:7], AF.Exp, scale=-0.5).then_inc(sS.h, 1)
            rstd_done = sS.inc()
            V.wait_ge(sS.h, rstd_done)
            V.tensor_scalar(stat[:, 8:9], stat[:, 2:3], stat[:, 7:8], -1.0,
                            ALU.mult, ALU.mult).then_inc(sV.h, 1)
            nb_done = sV.inc()
            V.wait_ge(sV.h, nb_done)
            if tt >= 1:
                V.wait_ge(sT.h, sT.n)
            V.tensor_scalar(xn2bf[:, 0:2048], resid[:, tt, :], stat[:, 7:8],
                            stat[:, 8:9], ALU.mult, ALU.add).then_inc(sV.h, 1)
            xn_done = sV.inc()
            for half in range(2):
                T.wait_ge(sV.h, xn_done)
                T.wait_ge(sV.h, sV.n)
                for g8 in range(8):
                    kt = half * 8 + g8
                    inst = T.transpose(pT[:, g8 * 128:(g8 + 1) * 128],
                                       xn2bf[:, kt * 128:(kt + 1) * 128], identb[:, :])
                inst.then_inc(sT.h, 1)
                tr_done = sT.inc()
                V.wait_ge(sT.h, tr_done)
                V.tensor_copy(xn2T[:, half * 8:(half + 1) * 8, 128 * tt: 128 * (tt + 1)],
                              pT[:, :].rearrange("p (g r) -> p g r", g=8)).then_inc(sV.h, 1)
                sV.inc()
        LN2_V = sV.n
        T.wait_ge(sV.h, LN2_V)
        for k in range(NKT):
            T.matmul(pE[0:1, :], onescol[:, 0:1], xn2T[:, k, 0:512], start=(k == 0), stop=(k == NKT - 1))
            T.matmul(pF[0:1, :], onescol[:, 0:1], xn2T[:, k, 512:1024], start=(k == 0), stop=(k == NKT - 1))
        T.ops[-1][3].append((sT.h, 1))
        cols_done = sT.inc()
        V.wait_ge(sT.h, cols_done)
        V.tensor_copy(colbf[0:1, 0:512], pE[0:1, :])
        V.tensor_copy(colbf[0:1, 512:1024], pF[0:1, :]).then_inc(sV.h, 1)
        colbf_v = sV.inc()

        hard_barrier(3)
        NR = 4
        IPR = 16
        hv_done = {}
        for rr in range(NR):
            for ii in range(IPR):
                it = rr * IPR + ii
                G.wait_ge(sV.h, colbf_v)
                if it >= 1:
                    G.wait_ge(sT.h, gmm_prev)
                G.dma_start(out=wgu[:, 0, :, :], in_=d_wg[:, it, :, :]).then_inc(sW0.h, 16)
                g_ld = sW0.inc(16)
                if it >= 1:
                    G.wait_ge(sT.h, umm_prev)
                G.dma_start(out=wgu[:, 1, :, :], in_=d_wu[:, it, :, :]).then_inc(sW1.h, 16)
                u_ld = sW1.inc(16)
                T.wait_ge(sW0.h, g_ld)
                T.wait_ge(sV.h, colbf_v)
                if it >= 1:
                    T.wait_ge(sS.h, sg_prev)
                for k in range(NKT):
                    st, sp = (k == 0), (k == NKT - 1)
                    T.matmul(pBIG[:, 0:512], wgu[:, 0, k, :], xn2T[:, k, 0:512], start=st, stop=sp)
                    T.matmul(pBIG[:, 512:1024], wgu[:, 0, k, :], xn2T[:, k, 512:1024], start=st, stop=sp)
                T.matmul(pBIG[:, 0:512], neg128[0:1, :], colbf[0:1, 0:512], start=False, stop=True)
                T.matmul(pBIG[:, 512:1024], neg128[0:1, :], colbf[0:1, 512:1024], start=False, stop=True)
                T.ops[-1][3].append((sT.h, 1))
                gmm = gmm_prev = sT.inc()
                T.wait_ge(sW1.h, u_ld)
                if it >= 1:
                    T.wait_ge(sV.h, hv_done[it - 1])
                for k in range(NKT):
                    st, sp = (k == 0), (k == NKT - 1)
                    T.matmul(pBIG[:, 1024:1536], wgu[:, 1, k, :], xn2T[:, k, 0:512], start=st, stop=sp)
                    T.matmul(pBIG[:, 1536:2048], wgu[:, 1, k, :], xn2T[:, k, 512:1024], start=st, stop=sp)
                T.matmul(pBIG[:, 1024:1536], neg128[0:1, :], colbf[0:1, 0:512], start=False, stop=True)
                T.matmul(pBIG[:, 1536:2048], neg128[0:1, :], colbf[0:1, 512:1024], start=False, stop=True)
                T.ops[-1][3].append((sT.h, 1))
                umm = umm_prev = sT.inc()
                S.wait_ge(sT.h, gmm)
                if it >= 1:
                    S.wait_ge(sV.h, hv_done[it - 1])
                S.activation(sgscr[:, :], pBIG[:, 0:1024], AF.Silu,
                             scale=gsc[:, it:it + 1]).then_inc(sS.h, 1)
                sg_done = sg_prev = sS.inc()
                V.wait_ge(sT.h, umm)
                V.wait_ge(sS.h, sg_done)
                V.scalar_tensor_tensor(hsb[:, ii, :], pBIG[:, 1024:2048], usc[:, it:it + 1],
                                       sgscr[:, :], ALU.mult, ALU.mult).then_inc(sV.h, 1)
                hv_done[it] = sV.inc()
            hard_barrier(4)
            T.wait_ge(sV.h, hv_done[rr * IPR + IPR - 1])
            for ii in range(IPR):
                T.matmul(pG[0:1, :], onescol[:, 0:1], hsb[:, ii, 0:512],
                         start=(ii == 0), stop=(ii == IPR - 1))
            T.ops[-1][3].append((sT.h, 1))
            hsA = sT.inc()
            V.wait_ge(sT.h, hsA)
            V.tensor_copy(hsbf[0:1, 0:512], pG[0:1, :]).then_inc(sV.h, 1)
            hsA_v = sV.inc()
            T.wait_ge(sV.h, hsA_v)
            for ii in range(IPR):
                T.matmul(pG[0:1, :], onescol[:, 0:1], hsb[:, ii, 512:1024],
                         start=(ii == 0), stop=(ii == IPR - 1))
            T.ops[-1][3].append((sT.h, 1))
            hsB = sT.inc()
            V.wait_ge(sT.h, hsB)
            V.tensor_copy(hsbf[0:1, 512:1024], pG[0:1, :]).then_inc(sV.h, 1)
            hs_v = sV.inc()
            for ht in range(16):
                ks = ht % 2
                sWk = sW2 if ks == 0 else sW3
                if rr + ht > 0:
                    G.wait_ge(sT.h, dmm_prev)
                G.dma_start(out=wdd[:, ks, :].rearrange("p (a b) -> p a b", a=16),
                            in_=d_wd[:, rr * IPR:(rr + 1) * IPR, 128 * ht: 128 * (ht + 1)]).then_inc(sWk.h, 16)
                wd_ld = sWk.inc(16)
                T.wait_ge(sWk.h, wd_ld)
                T.wait_ge(sV.h, hs_v)
                if rr + ht > 0:
                    T.wait_ge(sV.h, dse_prev)
                wdt = wdd[:, ks, :].rearrange("p (a b) -> p a b", a=16)
                for ki in range(IPR):
                    st = (ki == 0)
                    T.matmul(pE[:, :], wdt[:, ki, :], hsb[:, ki, 0:512], start=st, stop=False)
                    T.matmul(pF[:, :], wdt[:, ki, :], hsb[:, ki, 512:1024], start=st, stop=False)
                T.matmul(pE[:, :], neg128[0:1, :], hsbf[0:1, 0:512], start=False, stop=True)
                T.matmul(pF[:, :], neg128[0:1, :], hsbf[0:1, 512:1024], start=False, stop=True)
                T.ops[-1][3].append((sT.h, 1))
                dmm = dmm_prev = sT.inc()
                V.wait_ge(sT.h, dmm)
                V.wait_ge(sU.h, ds_ld)
                V.tensor_scalar(dwn[:, 0:512], pE[:, :], dscol[:, ht:ht + 1], None, ALU.mult)
                V.tensor_scalar(dwn[:, 512:1024], pF[:, :], dscol[:, ht:ht + 1], None, ALU.mult).then_inc(sV.h, 1)
                dse = sV.inc()
                T.wait_ge(sV.h, dse)
                if rr + ht > 0:
                    T.wait_ge(sV.h, radd_prev)
                for tt in range(8):
                    inst = T.transpose(pT[:, 128 * tt: 128 * (tt + 1)],
                                       dwn[:, 128 * tt: 128 * (tt + 1)], identb[:, :])
                inst.then_inc(sT.h, 1)
                dtr = sT.inc()
                V.wait_ge(sT.h, dtr)
                V.tensor_add(resid[:, :, 128 * ht: 128 * (ht + 1)],
                             resid[:, :, 128 * ht: 128 * (ht + 1)],
                             pT[:, :].rearrange("p (a b) -> p a b", a=8)).then_inc(sV.h, 1)
                dse_prev = radd_prev = sV.inc()
        P4_V = sV.n

        if DEBUG >= 4:
            G.wait_ge(sV.h, LN2_V)
            chain_dma(G, sO1, dbg["xn2T"], xn2T[:, :, :])
            G.wait_ge(sV.h, hv_done[0])
            chain_dma(G, sO1, dbg["h0"], hsb[:, 0, :])
        hard_barrier(5)
        G.wait_ge(sV.h, P4_V)
        for tt in range(8):
            chain_dma(G, sO, d_out[tt * 128:(tt + 1) * 128, :], resid[:, tt, :])
        G.wait_ge(sO.h, sO.n)
        if DEBUG:
            G.wait_ge(sO1.h, sO1.n)

    if PHASES < 4:
        V.wait_ge(sS.h, sS.n)
        V.wait_ge(sV.h, sV.n)
        V.memset(lnscr[:, 0, :], 0.0).then_inc(sV.h, 1)
        fin = sV.inc()
        G.wait_ge(sV.h, fin)
        for tt in range(8):
            chain_dma(G, sO, d_out[tt * 128:(tt + 1) * 128, :], lnscr[:, 0, :])
        G.wait_ge(sO.h, sO.n)
        if DEBUG:
            G.wait_ge(sO1.h, sO1.n)

    with nc.Block() as block:
        @block.sync
        def _(e):
            SY.replay(e)

        @block.tensor
        def _(e):
            T.replay(e)

        @block.vector
        def _(e):
            V.replay(e)

        @block.scalar
        def _(e):
            S.replay(e)

        @block.gpsimd
        def _(e):
            G.replay(e)

    for g in reversed(guards):
        g.__exit__(None, None, None)
    return nc



def _prep_shared(inputs):
    W = np.ascontiguousarray(np.asarray(inputs["L0_attn_in_proj_w"], dtype=np.float32))
    WzT = np.ascontiguousarray(W[:DI].T)
    WxT = np.ascontiguousarray(W[DI:].T)
    WxTp = np.zeros((D, 35 * 128), np.float32)
    WxTp[:, :4416] = WxT
    wx = np.ascontiguousarray(WxTp.reshape(NKT, 128, 35, 128).transpose(1, 2, 0, 3))
    wz = np.ascontiguousarray(WzT.reshape(NKT, 128, ZT, 128).transpose(1, 2, 0, 3))
    Wo = np.asarray(inputs["L0_attn_out_proj_w"], dtype=np.float32)
    wo = np.ascontiguousarray(Wo.T.reshape(ZT, 128, D).transpose(1, 0, 2))
    Wg = np.asarray(inputs["L0_ffn_gate_w"], dtype=np.int32)
    Wu = np.asarray(inputs["L0_ffn_up_w"], dtype=np.int32)
    Wd = np.asarray(inputs["L0_ffn_down_w"], dtype=np.int32)
    wg = np.ascontiguousarray(Wg.T.reshape(NKT, 128, IT, 128).transpose(1, 2, 0, 3))
    wu = np.ascontiguousarray(Wu.T.reshape(NKT, 128, IT, 128).transpose(1, 2, 0, 3))
    wd = np.ascontiguousarray(Wd.T.reshape(IT, 128, D).transpose(1, 0, 2))
    cw = np.ascontiguousarray(np.asarray(inputs["L0_attn_conv_w"], np.float32).reshape(XT, 128, 4).transpose(1, 0, 2))
    cb = np.ascontiguousarray(np.asarray(inputs["L0_attn_conv_b"], np.float32).reshape(XT, 128).T)
    dtbv = np.ascontiguousarray(np.asarray(inputs["L0_attn_dt_bias"], np.float32).reshape(64, 1))
    alogv = np.ascontiguousarray(np.asarray(inputs["L0_attn_A_log"], np.float32).reshape(64, 1))
    Dsk = np.asarray(inputs["L0_attn_D_skip"], np.float32)
    ddiag = np.zeros((128, ZT, 128), np.float32)
    for t in range(ZT):
        for k in range(128):
            ddiag[k, t, k] = Dsk[2 * t + (k >= 64)]
    nwv = np.ascontiguousarray(np.asarray(inputs["L0_attn_norm_w"], np.float32).reshape(ZT, 128).T)
    gs = np.ascontiguousarray(np.asarray(inputs["L0_ffn_gate_scale"], np.float32).reshape(IT, 128).T)
    us = np.ascontiguousarray(np.asarray(inputs["L0_ffn_up_scale"], np.float32).reshape(IT, 128).T)
    ds = np.ascontiguousarray(np.asarray(inputs["L0_ffn_down_scale"], np.float32).reshape(16, 128).T)
    identfv = np.eye(128, dtype=np.float32)
    tri = np.zeros((128, 2, CHUNK), np.float32)
    for half in range(2):
        for p in range(128):
            tri[p, half, half * 128 + p:] = 1.0
    return dict(wx=wx, wz=wz, wo=wo, wg=wg, wu=wu, wd=wd, convw=cw, convb=cb,
                dtb=dtbv, alog=alogv, ddiag=ddiag, nw=nwv, gscale=gs, uscale=us,
                dscale=ds, identf=identfv, tri=tri)


def _prep_core(inputs, core):
    b, q = core // 4, core % 4
    u = np.asarray(inputs["L0_attn_u"], np.float32)[b]
    t0 = q * LOCAL
    uh = np.zeros((TH, D), np.float32)
    if t0 >= HALO:
        uh[:HALO] = u[t0 - HALO:t0]
    uh[HALO:] = u[t0:t0 + LOCAL]
    Tm = np.zeros((8, 8), np.float32)
    Mm = np.zeros(8, np.float32)
    for r in range(8):
        rb, rq = r // 4, r % 4
        if rb == b and rq < q:
            Mm[r] = 1.0
            for m in range(8):
                mb, mq = m // 4, m % 4
                if mb == b and rq < mq < q:
                    Tm[r, m] = 1.0
    fA = Tm.copy()
    fB = 1.0 - Tm
    fA[:, 0] *= Mm
    fB[:, 0] *= Mm
    cmb = np.zeros((1, 128), np.float32)
    cmb[0, 0:64] = fA.reshape(-1)
    cmb[0, 64:128] = fB.reshape(-1)
    return dict(u=np.ascontiguousarray(uh), cmb=cmb)


_CACHE = {}


def kernel(**inputs):
    shared = _prep_shared(inputs)
    if "nc" not in _CACHE:
        _CACHE["nc"] = build_program()
    nc = _CACHE["nc"]
    in_maps = []
    for c in range(8):
        m = dict(shared)
        m.update(_prep_core(inputs, c))
        in_maps.append(m)
    res = run_bass_kernel_spmd(nc, in_maps, core_ids=list(range(8)))
    outs = [res.results[c]["out"] for c in range(8)]
    full = np.stack([np.concatenate(outs[0:4], 0), np.concatenate(outs[4:8], 0)])
    return full.astype(np.float32)


# version 04
# speedup vs baseline: 1.1107x; 1.1107x over previous
import numpy as np
import concourse.bass as bass
import concourse.mybir as mybir
from concourse.bass_utils import run_bass_kernel_spmd

F32 = mybir.dt.float32
BF16 = mybir.dt.bfloat16
I32 = mybir.dt.int32
AF = mybir.ActivationFunctionType
ALU = mybir.AluOpType
AX = mybir.AxisListType

D = 2048
DI = 4096
DS = 128
NH = 64
HP = 64
CHUNK = 256
NCK = 4
LOCAL = 1024
HALO = 3
TH = LOCAL + HALO
NKT = D // 128
XT = 34
ZT = 32
IT = 64
EPS = 1e-5

DEBUG = 0
PHASES = 4
BARRIERS = 0


class Rec:
    class _Inst:
        __slots__ = ("rec",)

        def __init__(self, rec):
            self.rec = rec

        def then_inc(self, sem, n=1):
            self.rec.ops[-1][3].append((sem, n))
            return self

    def __init__(self, name):
        self.name = name
        self.ops = []

    def op(self, method, *a, **k):
        self.ops.append((method, a, k, []))
        return Rec._Inst(self)

    def __getattr__(self, method):
        if method.startswith("_"):
            raise AttributeError(method)
        return lambda *a, **k: self.op(method, *a, **k)

    def replay(self, eng):
        for idx, (method, a, k, incs) in enumerate(self.ops):
            try:
                inst = getattr(eng, method)(*a, **k)
            except Exception as e:
                raise RuntimeError(f"{self.name} op#{idx} {method} args={[getattr(x, 'shape', x) for x in a]}: {e}") from e
            for sem, n in incs:
                inst.then_inc(sem, n)


class Sem:
    def __init__(self, handle):
        self.h = handle
        self.n = 0

    def inc(self, k=1):
        self.n += k
        return self.n


def build_program():
    nc = bass.Bass("TRN2", target_bir_lowering=False, detect_race_conditions=False)

    d_u = nc.dram_tensor("u", [TH, D], F32, kind="ExternalInput").ap()
    d_wx = nc.dram_tensor("wx", [128, 35, NKT, 128], F32, kind="ExternalInput").ap()
    d_wz = nc.dram_tensor("wz", [128, ZT, NKT, 128], F32, kind="ExternalInput").ap()
    d_wo = nc.dram_tensor("wo", [128, ZT, D], F32, kind="ExternalInput").ap()
    d_wg = nc.dram_tensor("wg", [128, IT, NKT, 128], I32, kind="ExternalInput").ap()
    d_wu = nc.dram_tensor("wu", [128, IT, NKT, 128], I32, kind="ExternalInput").ap()
    d_wd = nc.dram_tensor("wd", [128, IT, D], I32, kind="ExternalInput").ap()
    d_convw = nc.dram_tensor("convw", [128, XT, 4], F32, kind="ExternalInput").ap()
    d_convb = nc.dram_tensor("convb", [128, XT], F32, kind="ExternalInput").ap()
    d_dtb = nc.dram_tensor("dtb", [64, 1], F32, kind="ExternalInput").ap()
    d_alog = nc.dram_tensor("alog", [64, 1], F32, kind="ExternalInput").ap()
    d_ddiag = nc.dram_tensor("ddiag", [128, ZT, 128], F32, kind="ExternalInput").ap()
    d_nw = nc.dram_tensor("nw", [128, ZT], F32, kind="ExternalInput").ap()
    d_gs = nc.dram_tensor("gscale", [128, IT], F32, kind="ExternalInput").ap()
    d_us = nc.dram_tensor("uscale", [128, IT], F32, kind="ExternalInput").ap()
    d_ds = nc.dram_tensor("dscale", [128, 16], F32, kind="ExternalInput").ap()
    d_identf = nc.dram_tensor("identf", [128, 128], F32, kind="ExternalInput").ap()
    d_tri = nc.dram_tensor("tri", [128, 2, CHUNK], F32, kind="ExternalInput").ap()
    d_cmb = nc.dram_tensor("cmb", [1, 128], F32, kind="ExternalInput").ap()
    d_out = nc.dram_tensor("out", [LOCAL, D], F32, kind="ExternalOutput").ap()
    d_xnsp = nc.dram_tensor("xnsp", [128, NKT, 1032], BF16).ap()
    d_stsp = nc.dram_tensor("stsp", [NCK, 128, DI], BF16).ap()
    d_rssp = nc.dram_tensor("rssp", [8, 128, D], F32).ap()
    cc_in = nc.dram_tensor("cc_in", [129, DI], BF16)
    cc_out = nc.dram_tensor("cc_out", [8 * 129, DI], BF16)

    dbg = {}

    def dbgt(name, shape, dtype=F32):
        dbg[name] = nc.dram_tensor("dbg_" + name, shape, dtype, kind="ExternalOutput").ap()

    if DEBUG >= 1:
        dbgt("stat0", [128, 16])
        dbgt("xnbf0", [128, 2, LOCAL], BF16)
        dbgt("xnT", [128, NKT, TH], BF16)
        dbgt("x_fm", [128, ZT, LOCAL], BF16)
        dbgt("BT", [128, LOCAL], BF16)
        dbgt("CT", [128, LOCAL], BF16)
        dbgt("dt", [64, LOCAL])
    if DEBUG >= 2:
        dbgt("cs", [64, NCK, CHUNK])
        dbgt("sloc", [128, DI], BF16)
        dbgt("sin", [128, DI])
        dbgt("y", [128, ZT, LOCAL], BF16)
    if DEBUG >= 3:
        dbgt("xnT2", [128, NKT, 1032], BF16)
        dbgt("zlast", [128, LOCAL], BF16)
        dbgt("sqlast", [128, LOCAL], BF16)
        dbgt("ygated", [128, ZT, LOCAL], BF16)
        dbgt("inv", [128, 8])
        dbgt("resid", [8, 128, D])
    if DEBUG >= 4:
        dbgt("xn2T", [128, NKT, LOCAL], BF16)
        dbgt("h0", [128, LOCAL], BF16)

    SBASE = 17408
    BUDGET = 229376

    def at(name, shape, dtype, off):
        off = off + SBASE
        nbytes = int(np.prod(shape[1:])) * np.dtype(mybir.dt.np(dtype)).itemsize
        assert off + nbytes <= BUDGET, (name, off, nbytes)
        return nc.alloc_sbuf_tensor_at(name, shape, dtype, offset=off).ap()

    x_fm = at("x_fm", [128, ZT, LOCAL], BF16, 0)
    resid = at("resid", [128, 8, D], F32, 0)
    BT = at("BT", [128, LOCAL], BF16, 65536)
    CT = at("CT", [128, LOCAL], BF16, 67584)
    hsb = at("hsb", [128, 16, LOCAL], BF16, 65536)
    identb = at("identb", [128, 128], BF16, 98304)
    identf = at("identf_s", [128, 128], F32, 98560)
    trib = at("trib", [128, 2, CHUNK], BF16, 99072)
    convw = at("convw_s", [128, XT, 4], F32, 100096)
    convb = at("convb_s", [128, XT], F32, 100672)
    nw = at("nw_s", [128, ZT], F32, 100832)
    gsc = at("gsc", [128, IT], F32, 100960)
    usc = at("usc", [128, IT], F32, 101216)
    dtb = at("dtb_s", [64, 1], F32, 101472)
    alog = at("alog_s", [64, 1], F32, 101504)
    acol = at("acol", [64, 1], F32, 101536)
    dtot = at("dtot", [64, 1], F32, 101568)
    ones1 = at("ones1", [1, 128], BF16, 101600)
    onescol = at("onescol", [128, 1], BF16, 101856)
    neg128 = at("neg128", [1, 128], BF16, 101888)
    small = at("small", [128, 64], F32, 102144)
    stat = at("stat", [128, 16], F32, 102400)
    wbb = at("wbb", [128, 8, 64], F32, 102464)
    cmb = at("cmb_s", [1, 128], F32, 104512)
    onesf = at("onesf", [1, 128], F32, 105024)
    csbf = at("csbf", [64, 1], BF16, 105536)
    csrow = at("csrow", [1, 1024], F32, 105600)
    colbf = at("colbf", [1, 1024], BF16, 109696)
    hsbf = at("hsbf", [1, 1024], BF16, 116736)
    dscol = at("dscol", [128, 16], F32, 112640)
    sgscr = at("sgscr", [128, LOCAL], BF16, 191104 + 16384)
    dwn = at("dwn", [128, LOCAL], BF16, 191104 + 18432)
    drow = at("drow", [1, 512], F32, 77824)
    wrow = at("wrow", [1, 512], F32, 79872)
    ddiag = at("ddiag_s", [128, ZT, 128], BF16, 69632)
    dt_t = at("dt_t", [64, LOCAL], F32, 112640)
    dA = at("dA", [64, LOCAL], F32, 116736)
    cs = at("cs_s", [64, NCK, CHUNK], F32, 120832)
    csT = at("csT", [128, NCK, 2, 64], F32, 124928)
    dtT = at("dtT", [128, NCK, 2, 64], F32, 126976)
    ddecT = at("ddecT", [128, NCK, 2, 64], F32, 129024)
    cdb = at("cdb", [128, NCK, 64], F32, 131072)
    X0 = 132608
    xnT = at("xnT", [128, NKT, 1032], BF16, X0)
    segscr = at("segscr", [128, 4, 2, CHUNK], F32, X0)
    Mt = at("Mt", [128, 4, 2, CHUNK], BF16, X0 + 8192)
    CBm = at("CBm", [128, 2, CHUNK], BF16, X0 + 12288)
    Btok = at("Btok", [128, 2, 128], BF16, 111744)
    sgbuf = at("sgbuf", [128, D], BF16, X0 + 13312)
    Eq = at("Eq", [128, 8, CHUNK], BF16, X0 + 2048)
    yoffq = at("yoffq", [128, 8, CHUNK], BF16, X0 + 6144)
    xdt = at("xdt", [128, 2, DI], BF16, X0 + 17408)
    xn2T = at("xn2T", [128, NKT, LOCAL], BF16, X0)
    Y0 = 166400
    lnscr = at("lnscr", [128, 2, D], F32, Y0)
    lnsq = at("lnsq", [128, D], F32, Y0 + 16384)
    prev = at("prev", [128, DI], F32, Y0)
    prev_bf = at("prev_bf", [128, DI], BF16, Y0 + 16384)
    ubuf = at("ubuf", [128, 2, D], F32, Y0)
    pscr = at("pscr", [128, D], F32, Y0 + 16384)
    lnsq2 = at("lnsq2", [128, D], F32, Y0 + 8192)
    ubuf4 = at("ubuf4", [128, D], F32, Y0)
    Z0 = 191104
    xnbf = at("xnbf", [128, 2, D], BF16, Z0)
    convscr = at("convscr", [128, 2, 1030], F32, Z0)
    wring = at("wring", [128, 2, NKT, 128], BF16, Z0 + 8256)
    dtraw = at("dtraw", [64, TH], F32, Z0 + 16448)
    xdd2 = at("xdd2", [128, 2, DI], BF16, Z0)
    zsb = at("zsb", [128, LOCAL], BF16, Z0)
    sqsb = at("sqsb", [128, LOCAL], BF16, Z0 + 2048)
    wdoB = at("wdoB", [128, 2, ZT, 128], BF16, Z0 + 4096)
    wgu = at("wgu", [128, 2, NKT, 128], BF16, Z0)
    wdd = at("wdd", [128, 2, D], BF16, Z0 + 8192)

    pBIG = nc.alloc_psum_tensor("pBIG", [128, 2048], F32).ap()
    pE = nc.alloc_psum_tensor("pE", [128, 512], F32).ap()
    pF = nc.alloc_psum_tensor("pF", [128, 512], F32).ap()
    pG = nc.alloc_psum_tensor("pG", [128, 512], F32).ap()
    pT = nc.alloc_psum_tensor("pT", [128, 1024], BF16).ap()

    guards = []

    def newsem(nm):
        g = nc.semaphore(nm)
        h = g.__enter__()
        guards.append(g)
        return Sem(h)

    sT = newsem("sT")
    sV = newsem("sV")
    sS = newsem("sS")
    sG = newsem("sG")
    sC = newsem("sC")
    sU = newsem("sU")
    sW = newsem("sW")
    sU0 = newsem("sU0")
    sU1 = newsem("sU1")
    sW0 = newsem("sW0")
    sW1 = newsem("sW1")
    sW2 = newsem("sW2")
    sW3 = newsem("sW3")
    sO = newsem("sO")
    sO1 = newsem("sO1")
    sP = newsem("sP")
    sQ = newsem("sQ")

    T, V, S, G, SY = Rec("tensor"), Rec("vector"), Rec("scalar"), Rec("gpsimd"), Rec("sync")

    def hard_barrier(bit):
        if not (BARRIERS & (1 << bit)):
            return
        for E in (T, V, S, G, SY):
            for sm in (sT, sV, sS, sG, sU, sW, sU0, sU1, sW0, sW1, sW2, sW3, sO, sO1, sP, sQ):
                if sm.n:
                    E.wait_ge(sm.h, sm.n)

    def chain_dma(R, sem, out, in_):
        if sem.n:
            R.wait_ge(sem.h, sem.n)
        R.dma_start(out=out, in_=in_).then_inc(sem.h, 16)
        return sem.inc(16)

    chain_dma(G, sW, identb[:, :], d_identf)
    chain_dma(G, sW, trib[:, :, :], d_tri)
    chain_dma(G, sW, ddiag[:, :, :], d_ddiag)
    chain_dma(SY, sU, identf[:, :], d_identf)
    chain_dma(SY, sU, convw[:, :, :], d_convw)
    chain_dma(SY, sU, convb[:, :], d_convb)
    chain_dma(SY, sU, dtb[:, :], d_dtb)
    chain_dma(SY, sU, alog[:, :], d_alog)
    chain_dma(SY, sU, nw[:, :], d_nw)
    chain_dma(SY, sU, gsc[:, :], d_gs)
    chain_dma(SY, sU, usc[:, :], d_us)
    chain_dma(SY, sU, cmb[:, :], d_cmb)
    CONST_U = sU.n
    CONST_W = sW.n
    V.wait_ge(sU.h, CONST_U)
    V.wait_ge(sW.h, CONST_W)
    V.memset(ones1[:, :], 1.0)
    V.memset(onescol[:, :], 1.0)
    V.memset(neg128[:, :], -128.0)
    V.memset(onesf[:, :], 1.0).then_inc(sV.h, 1)
    V_CONST = sV.inc()
    S.wait_ge(sU.h, CONST_U)
    S.activation(acol[:, :], alog[:, :], AF.Exp).then_inc(sS.h, 1)
    S_ACOL = sS.inc()

    import os
    NUT = int(os.environ.get("NUT", "9"))
    last_tr = {}
    for ut in range(NUT):
        p = ut % 2
        rows = 128 if ut < 8 else TH - 8 * 128
        sUp = sU0 if p == 0 else sU1
        if ut >= 2:
            SY.wait_ge(sS.h, sS.n)
            SY.wait_ge(sV.h, sV.n)
        SY.dma_start(out=lnscr[0:rows, p, :], in_=d_u[ut * 128: ut * 128 + rows, :]).then_inc(sUp.h, 16)
        u_ld = sUp.inc(16)
        V.wait_ge(sUp.h, u_ld)
        V.reduce_sum(stat[0:rows, 0:1], lnscr[0:rows, p, :], axis=AX.X).then_inc(sV.h, 1)
        sum_done = sV.inc()
        V.memset(stat[0:rows, 1:2], 0.0).then_inc(sV.h, 1)
        sqz = sV.inc()
        S.wait_ge(sUp.h, u_ld)
        S.wait_ge(sV.h, sqz)
        S.activation(lnsq[0:rows, :], lnscr[0:rows, p, :], AF.Square,
                     accum_out=stat[0:rows, 1:2]).then_inc(sS.h, 1)
        sq_done = sS.inc()
        V.wait_ge(sS.h, sq_done)
        V.tensor_scalar(stat[0:rows, 2:3], stat[0:rows, 0:1], 1.0 / D, None, ALU.mult)
        V.tensor_scalar(stat[0:rows, 3:4], stat[0:rows, 1:2], 1.0 / D, EPS, ALU.mult, ALU.add).then_inc(sV.h, 1)
        V.wait_ge(sV.h, sV.inc())
        V.tensor_scalar(stat[0:rows, 4:5], stat[0:rows, 2:3], stat[0:rows, 2:3], None, ALU.mult).then_inc(sV.h, 1)
        V.wait_ge(sV.h, sV.inc())
        V.tensor_sub(stat[0:rows, 5:6], stat[0:rows, 3:4], stat[0:rows, 4:5]).then_inc(sV.h, 1)
        var_done = sV.inc()
        S.wait_ge(sV.h, var_done)
        S.activation(stat[0:rows, 6:7], stat[0:rows, 5:6], AF.Ln).then_inc(sS.h, 1)
        S.wait_ge(sS.h, sS.inc())
        S.activation(stat[0:rows, 7:8], stat[0:rows, 6:7], AF.Exp, scale=-0.5).then_inc(sS.h, 1)
        rstd_done = sS.inc()
        V.wait_ge(sS.h, rstd_done)
        V.tensor_scalar(stat[0:rows, 8:9], stat[0:rows, 2:3], stat[0:rows, 7:8], -1.0,
                        ALU.mult, ALU.mult).then_inc(sV.h, 1)
        nb_done = sV.inc()
        V.wait_ge(sV.h, nb_done)
        if ut >= 2:
            V.wait_ge(sT.h, last_tr[ut - 2])
        V.tensor_scalar(xnbf[0:rows, p, :], lnscr[0:rows, p, :], stat[0:rows, 7:8],
                        stat[0:rows, 8:9], ALU.mult, ALU.add).then_inc(sV.h, 1)
        xn_done = sV.inc()
        if DEBUG >= 1 and ut == 0:
            V.tensor_copy(small[:, 0:16], stat[:, :])
        if DEBUG >= 1 and ut == 1:
            V.tensor_copy(x_fm[:, 0:2, :].rearrange("p a b -> p (a b)"), xnbf[:, 0, :])
        slotw = rows if rows % 2 == 0 else rows + 1
        for half in range(2):
            T.wait_ge(sV.h, xn_done)
            if (ut, half) != (0, 0):
                T.wait_ge(sV.h, sV.n)
            for g8 in range(8):
                kt = half * 8 + g8
                inst = T.transpose(pT[:, g8 * slotw:g8 * slotw + rows],
                                   xnbf[0:rows, p, kt * 128:(kt + 1) * 128], identb[0:rows, 0:rows])
            inst.then_inc(sT.h, 1)
            tr_done = sT.inc()
            last_tr[ut] = tr_done
            V.wait_ge(sT.h, tr_done)
            V.tensor_copy(xnT[:, half * 8:(half + 1) * 8, ut * 128: ut * 128 + rows],
                          pT[:, 0:8 * slotw].rearrange("p (g r) -> p g r", g=8)[:, :, 0:rows]).then_inc(sV.h, 1)
            sV.inc()
    LN1_V = sV.n

    if PHASES >= 1:
      T.wait_ge(sV.h, LN1_V)
      evac_done = [None, None]
      silu_done = [None, None]
      mm_done_list = []
      for mt in range(35):
          slot = mt % 2
          sWs = sW0 if slot == 0 else sW1
          if mt >= 2:
              G.wait_ge(sT.h, mm_done_list[mt - 2])
          G.dma_start(out=wring[:, slot, :, :], in_=d_wx[:, mt, :, :]).then_inc(sWs.h, 16)
          w_ld = sWs.inc(16)
          if slot == 0:
              psA, psB, psC = pBIG[:, 0:512], pBIG[:, 512:1024], pG[:, 0:3]
          else:
              psA, psB, psC = pE[:, :], pF[:, :], pG[:, 256:259]
          T.wait_ge(sWs.h, w_ld)
          if evac_done[slot] is not None:
              T.wait_ge(sV.h, evac_done[slot])
          for k in range(NKT):
              st, sp = (k == 0), (k == NKT - 1)
              T.matmul(psA, wring[:, slot, k, :], xnT[:, k, 0:512], start=st, stop=sp)
              T.matmul(psB, wring[:, slot, k, :], xnT[:, k, 512:1024], start=st, stop=sp)
          if evac_done[1 - slot] is not None:
              T.wait_ge(sV.h, evac_done[1 - slot])
          for k in range(NKT):
              inst = T.matmul(psC, wring[:, slot, k, :], xnT[:, k, 1024:1027],
                              start=(k == 0), stop=(k == NKT - 1))
          inst.then_inc(sT.h, 1)
          mm_done = sT.inc()
          mm_done_list.append(mm_done)
          V.wait_ge(sT.h, mm_done)
          if mt < 34:
              if silu_done[slot] is not None:
                  V.wait_ge(sS.h, silu_done[slot])
              V.tensor_copy(convscr[:, slot, 0:512], psA)
              V.tensor_copy(convscr[:, slot, 512:1024], psB)
              V.tensor_copy(convscr[:, slot, 1024:1027], psC).then_inc(sV.h, 1)
              evac_done[slot] = sV.inc()
              V.tensor_scalar(lnsq[:, 0:1024], convscr[:, slot, 0:1024], convw[:, mt, 0:1], None, ALU.mult)
              for kk in range(1, 3):
                  V.scalar_tensor_tensor(lnsq[:, 0:1024], convscr[:, slot, kk:kk + 1024],
                                         convw[:, mt, kk:kk + 1], lnsq[:, 0:1024], ALU.mult, ALU.add)
              V.scalar_tensor_tensor(lnsq[:, 0:1024], convscr[:, slot, 3:3 + 1024],
                                     convw[:, mt, 3:4], lnsq[:, 0:1024], ALU.mult, ALU.add).then_inc(sV.h, 1)
              conv_done = sV.inc()
              S.wait_ge(sV.h, conv_done)
              dest = x_fm[:, mt, :] if mt < 32 else (BT[:, :] if mt == 32 else CT[:, :])
              S.activation(dest, lnsq[:, 0:1024], AF.Silu, bias=convb[:, mt:mt + 1]).then_inc(sS.h, 1)
              silu_done[slot] = sS.inc()
          else:
              V.tensor_copy(dtraw[:, 0:512], psA[0:64, :])
              V.tensor_copy(dtraw[:, 512:1024], psB[0:64, :])
              V.tensor_copy(dtraw[:, 1024:1027], psC[0:64, :]).then_inc(sV.h, 1)
              evac_done[slot] = sV.inc()
              S.wait_ge(sV.h, evac_done[slot])
              S.activation(lnsq[0:64, 0:1024], dtraw[:, HALO:TH], AF.Relu, bias=dtb[:, 0:1])
              S.activation(lnsq[0:64, 1024:2048], dtraw[:, HALO:TH], AF.Abs, bias=dtb[:, 0:1])
              S.activation(lnsq[0:64, 1024:2048], lnsq[0:64, 1024:2048], AF.Exp, scale=-1.0)
              S.activation(lnsq[0:64, 1024:2048], lnsq[0:64, 1024:2048], AF.Ln, bias=1.0).then_inc(sS.h, 1)
              sp_done = sS.inc()
              V.wait_ge(sS.h, sp_done)
              V.tensor_add(dt_t[:, :], lnsq[0:64, 0:1024], lnsq[0:64, 1024:2048]).then_inc(sV.h, 1)
              sV.inc()
              V.wait_ge(sS.h, S_ACOL)
              V.tensor_scalar(dA[:, :], dt_t[:, :], acol[:, 0:1], -1.0, ALU.mult, ALU.mult).then_inc(sV.h, 1)
              sV.inc()
      P1_V = sV.n
      P1_S = sS.n
      P1_T = sT.n

    if DEBUG >= 1:
        G.wait_ge(sV.h, sV.n)
        G.wait_ge(sS.h, sS.n)
        chain_dma(G, sO1, dbg["stat0"], small[:, 0:16])
        chain_dma(G, sO1, dbg["xnbf0"], x_fm[:, 0:2, :])
        chain_dma(G, sO1, dbg["xnT"], xnT[:, :, 0:TH])
        if PHASES >= 1:
            chain_dma(G, sO1, dbg["x_fm"], x_fm[:, :, :])
            chain_dma(G, sO1, dbg["BT"], BT[:, :])
            chain_dma(G, sO1, dbg["CT"], CT[:, :])
            chain_dma(G, sO1, dbg["dt"], dt_t[:, :])


    if PHASES >= 2:
        CSL = CHUNK - 1

        def vedge():
            V.ops[-1][3].append((sV.h, 1))
            V.wait_ge(sV.h, sV.inc())

        G.wait_ge(sV.h, sV.n)
        G.wait_ge(sS.h, sS.n)
        chain_dma(G, sP, d_xnsp, xnT[:, :, :])
        xnsp_done = sP.n
        V.wait_ge(sP.h, xnsp_done)
        if DEBUG:
            V.wait_ge(sO1.h, sO1.n)
        dA4 = dA[:, :].rearrange("p (c l) -> p c l", c=NCK)
        cs4 = cs[:, :, :]
        sg4 = segscr[0:64, 0:2, :, :].rearrange("p a b l -> p (a b) l")
        src4 = dA4
        dst4 = cs4
        k = 1
        step = 0
        while k < CHUNK:
            V.tensor_copy(dst4[:, :, 0:k], src4[:, :, 0:k])
            V.tensor_add(dst4[:, :, k:CHUNK], src4[:, :, k:CHUNK], src4[:, :, 0:CHUNK - k])
            vedge()
            src4, dst4 = dst4, (sg4 if step % 2 == 0 else cs4)
            k *= 2
            step += 1
        V.tensor_copy(cs4[:, :, :], sg4[:, :, :])
        vedge()
        CS_DONE = sV.n
        V.reduce_sum(dtot[:, 0:1], dA[:, :], axis=AX.X).then_inc(sV.h, 1)
        DTOT_V = sV.inc()
        S.wait_ge(sV.h, DTOT_V)
        S.activation(dtot[:, 0:1], dtot[:, 0:1], AF.Exp).then_inc(sS.h, 1)
        DTOT_S = sS.inc()
        V.memset(prev[:, :], 0.0).then_inc(sV.h, 1)
        PREV0 = sV.inc()

        st_spill = []
        msub_done = {}
        exp_done = {}
        gm_done = {}
        ymm_done = {}
        ypair_mm = {}
        for c in range(NCK):
            t0 = c * CHUNK
            csc = cs[:, c, :]
            T.wait_ge(sV.h, CS_DONE)
            if c > 0:
                T.wait_ge(sV.h, sV.n)
            for sh in range(2):
                T.transpose(pG[:, 0:64], csc[:, sh * 128:(sh + 1) * 128], identf[0:64, 0:64])
                T.ops[-1][3].append((sT.h, 1))
                trd = sT.inc()
                V.wait_ge(sT.h, trd)
                V.tensor_copy(csT[:, c, sh, :], pG[:, 0:64]).then_inc(sV.h, 1)
                sV.inc()
                T.wait_ge(sV.h, sV.n)
                T.transpose(pG[:, 0:64], dt_t[:, t0 + sh * 128: t0 + (sh + 1) * 128],
                            identf[0:64, 0:64])
                T.ops[-1][3].append((sT.h, 1))
                trd = sT.inc()
                V.wait_ge(sT.h, trd)
                V.tensor_copy(dtT[:, c, sh, :], pG[:, 0:64]).then_inc(sV.h, 1)
                sV.inc()
                T.wait_ge(sV.h, sV.n)
            e1s = segscr[0:64, 1, 0, :]
            dds = segscr[0:64, 1, 1, :]
            S.wait_ge(sV.h, CS_DONE)
            S.activation(e1s, csc[:, :], AF.Exp, scale=-1.0, bias=csc[:, CSL:CSL + 1]).then_inc(sS.h, 1)
            e1_done = sS.inc()
            V.wait_ge(sS.h, e1_done)
            V.tensor_mul(dds, dt_t[:, t0:t0 + CHUNK], e1s)
            vedge()
            for sh in range(2):
                T.wait_ge(sV.h, sV.n)
                T.transpose(pG[:, 64:128], dds[:, sh * 128:(sh + 1) * 128], identf[0:64, 0:64])
                T.ops[-1][3].append((sT.h, 1))
                trd = sT.inc()
                V.wait_ge(sT.h, trd)
                V.tensor_copy(ddecT[:, c, sh, :], pG[:, 64:128]).then_inc(sV.h, 1)
                sV.inc()
            T.wait_ge(sV.h, sV.n)
            T.transpose(pG[0:1, 128:192], csc[:, CSL:CSL + 1], identf[0:64, 0:64])
            T.ops[-1][3].append((sT.h, 1))
            trd = sT.inc()
            S.wait_ge(sT.h, trd)
            S.activation(small[0:1, 0:64], pG[0:1, 128:192], AF.Exp).then_inc(sS.h, 1)
            cdrow_done = sS.inc()
            T.wait_ge(sS.h, cdrow_done)
            T.matmul(pG[:, 192:256], onesf[:, :], small[0:1, 0:64], start=True, stop=True)
            T.ops[-1][3].append((sT.h, 1))
            trd = sT.inc()
            V.wait_ge(sT.h, trd)
            V.tensor_copy(cdb[:, c, :], pG[:, 192:256]).then_inc(sV.h, 1)
            sV.inc()

            T.wait_ge(sS.h, sS.n)
            for sh in range(2):
                T.matmul(pF[:, sh * 256:(sh + 1) * 256], BT[:, t0 + sh * 128: t0 + (sh + 1) * 128],
                         CT[:, t0:t0 + CHUNK], start=True, stop=True)
            T.ops[-1][3].append((sT.h, 1))
            cb_done = sT.inc()
            V.wait_ge(sT.h, cb_done)
            V.tensor_mul(CBm[:, :, :].rearrange("p a l -> p (a l)"), pF[:, :],
                         trib[:, :, :].rearrange("p a l -> p (a l)")).then_inc(sV.h, 1)
            cbm_done = sV.inc()

            for sh in range(2):
                for gq in range(4):
                    tiles = range(gq * 8, (gq + 1) * 8)
                    if not (c == 0 and sh == 0 and gq == 0):
                        T.wait_ge(sV.h, sV.n)
                    for j, t in enumerate(tiles):
                        inst = T.transpose(pT[:, j * 128:(j + 1) * 128],
                                           x_fm[:, t, t0 + sh * 128: t0 + (sh + 1) * 128],
                                           identb[:, :])
                    inst.then_inc(sT.h, 1)
                    trd = sT.inc()
                    V.wait_ge(sT.h, trd)
                    dtbc = dtT[:, c, sh, 16 * gq:16 * gq + 16].unsqueeze(2).broadcast_to([128, 16, 64])
                    ddbc = ddecT[:, c, sh, 16 * gq:16 * gq + 16].unsqueeze(2).broadcast_to([128, 16, 64])
                    pTv = pT[:, 0:1024].rearrange("p (a b) -> p a b", a=16)
                    V.tensor_tensor(xdt[:, sh, 1024 * gq:1024 * (gq + 1)].rearrange("p (a b) -> p a b", a=16),
                                    pTv, dtbc, ALU.mult)
                    V.tensor_tensor(xdd2[:, sh, 1024 * gq:1024 * (gq + 1)].rearrange("p (a b) -> p a b", a=16),
                                    pTv, ddbc, ALU.mult).then_inc(sV.h, 1)
                    sV.inc()
                T.wait_ge(sV.h, sV.n)
                T.transpose(pT[:, 0:128], BT[:, t0 + sh * 128: t0 + (sh + 1) * 128], identb[:, :])
                T.ops[-1][3].append((sT.h, 1))
                trd = sT.inc()
                V.wait_ge(sT.h, trd)
                V.tensor_copy(Btok[:, sh, :], pT[:, 0:128]).then_inc(sV.h, 1)
                btok_done = sV.inc()
            for j in range(8):
                ps = pE if j % 2 == 0 else pF
                T.wait_ge(sV.h, btok_done)
                if c + j > 0:
                    T.wait_ge(sV.h, sV.n)
                for sh in range(2):
                    T.matmul(ps[:, :], Btok[:, sh, :], xdd2[:, sh, j * 512:(j + 1) * 512],
                             start=(sh == 0), stop=(sh == 1))
                T.ops[-1][3].append((sT.h, 1))
                st_done = sT.inc()
                V.wait_ge(sT.h, st_done)
                if j % 4 == 0 and (c > 0 or j > 0):
                    V.wait_ge(sP.h, sP.n)
                V.tensor_copy(sgbuf[:, (j % 4) * 512:(j % 4 + 1) * 512], ps[:, :])
                vedge()
                if c == 0:
                    V.tensor_copy(prev[:, j * 512:(j + 1) * 512], ps[:, :]).then_inc(sV.h, 1)
                    sV.inc()
                else:
                    cdbc = cdb[:, c, 8 * j:8 * j + 8].unsqueeze(2).broadcast_to([128, 8, 64])
                    prv = prev[:, j * 512:(j + 1) * 512].rearrange("p (a b) -> p a b", a=8)
                    V.tensor_tensor(prv, prv, cdbc, ALU.mult)
                    vedge()
                    V.tensor_tensor(prev[:, j * 512:(j + 1) * 512],
                                    prev[:, j * 512:(j + 1) * 512], ps[:, :], ALU.add).then_inc(sV.h, 1)
                    sV.inc()
                if j % 4 == 3:
                    G.wait_ge(sV.h, sV.n)
                    chain_dma(G, sP, d_stsp[c].rearrange("p (a b) -> p a b", a=2)[:, j // 4, :],
                              sgbuf[:, :])
            st_spill.append(sP.n)

            if c == 0:
                T.wait_ge(sV.h, sV.n)
            for h in range(NH):
                hg = c * NH + h
                q = h % 4
                pr = h // 2
                prs = pr % 8
                pbc = (pG if q < 2 else pE)[:, 256 * (q % 2): 256 * (q % 2 + 1)]
                if hg >= 4:
                    T.wait_ge(sV.h, msub_done[hg - 4])
                else:
                    T.wait_ge(sV.h, CS_DONE)
                oh = identf[0:64, h:h + 1].broadcast_to([64, 128])
                T.matmul(pbc, oh, csc[:, :], start=True, stop=True)
                T.ops[-1][3].append((sT.h, 1))
                bc_done = sT.inc()
                V.wait_ge(sT.h, bc_done)
                if hg >= 4:
                    V.wait_ge(sS.h, exp_done[hg - 4])
                for sh in range(2):
                    V.scalar_tensor_tensor(segscr[:, q, sh, :], pbc, csT[:, c, sh, h:h + 1],
                                           trib[:, sh, :], ALU.subtract, ALU.mult)
                V.ops[-1][3].append((sV.h, 1))
                msub_done[hg] = sV.inc()
                S.wait_ge(sV.h, msub_done[hg])
                if hg >= 4:
                    S.wait_ge(sG.h, gm_done[hg - 4])
                    S.wait_ge(sT.h, ymm_done[hg - 4])
                S.activation(Mt[:, q, :, :].rearrange("p a l -> p (a l)"),
                             segscr[:, q, :, :].rearrange("p a l -> p (a l)"),
                             AF.Exp).then_inc(sS.h, 1)
                exp_done[hg] = sS.inc()
                G.wait_ge(sS.h, exp_done[hg])
                G.wait_ge(sV.h, cbm_done)
                G.tensor_tensor(Mt[:, q, :, :].rearrange("p a l -> p (a l)"),
                                Mt[:, q, :, :].rearrange("p a l -> p (a l)"),
                                CBm[:, :, :].rearrange("p a l -> p (a l)"), ALU.mult).then_inc(sG.h, 1)
                gm_done[hg] = sG.inc()
                T.wait_ge(sG.h, gm_done[hg])
                pY = pBIG[:, 256 * prs: 256 * (prs + 1)]
                hq = h % 2
                if hq == 0:
                    if prs in ypair_mm:
                        T.wait_ge(sV.h, ypair_mm[prs])
                    T.matmul(pY, ddiag[:, pr, :], x_fm[:, pr, t0:t0 + CHUNK],
                             start=True, stop=True)
                for sh in range(2):
                    T.matmul(pY[64 * hq:64 * (hq + 1), :], xdt[:, sh, 64 * h: 64 * (h + 1)],
                             Mt[:, q, sh, :], start=False, stop=True,
                             tile_position=(0, 64 * hq))
                T.ops[-1][3].append((sT.h, 1))
                ymm_done[hg] = sT.inc()
                if hq == 1 and pr % 2 == 1:
                    V.wait_ge(sT.h, ymm_done[hg])
                    pslot0 = (pr - 1) % 8
                    V.tensor_copy(
                        x_fm[:, pr - 1: pr + 1, t0:t0 + CHUNK],
                        pBIG[:, 256 * pslot0: 256 * (pslot0 + 2)].rearrange("p (a l) -> p a l", a=2),
                    ).then_inc(sV.h, 1)
                    ypair_mm[pslot0] = sV.inc()
                    ypair_mm[(pr) % 8] = ypair_mm[pslot0]
        P2A_V = sV.n

        V.wait_ge(sV.h, P2A_V)
        V.wait_ge(sP.h, st_spill[-1])
        for _hf in range(2):
            if _hf:
                V.wait_ge(sQ.h, sQ.n)
            V.tensor_copy(sgbuf[:, :], prev[:, 2048 * _hf: 2048 * (_hf + 1)]).then_inc(sV.h, 1)
            _sl = sV.inc()
            G.wait_ge(sV.h, _sl)
            chain_dma(G, sQ, cc_in.ap()[0:128, 2048 * _hf: 2048 * (_hf + 1)], sgbuf[:, :])
        sloc_v = sV.n
        T.wait_ge(sS.h, DTOT_S)
        T.transpose(pG[0:1, 448:512], dtot[:, 0:1], identf[0:64, 0:64])
        T.ops[-1][3].append((sT.h, 1))
        dtr_done = sT.inc()
        V.wait_ge(sT.h, dtr_done)
        V.tensor_copy(csbf[0:1, 0:1].rearrange("p a -> p a") if False else Btok[0:1, 0, 0:64],
                      pG[0:1, 448:512]).then_inc(sV.h, 1)
        dtrow_v = sV.inc()
        G.wait_ge(sV.h, dtrow_v)
        chain_dma(G, sQ, cc_in.ap()[128:129, 0:64], Btok[0:1, 0, 0:64])
        G.collective_compute(
            "AllGather", ALU.bypass,
            replica_groups=[list(range(8))],
            ins=[cc_in.ap().opt()],
            outs=[cc_out.ap().opt()],
        ).then_inc(sC.h, 1)
        sC.inc()

        if DEBUG >= 2:
            G.wait_ge(sV.h, sV.n)
            G.wait_ge(sQ.h, sQ.n)
            chain_dma(G, sO1, dbg["cs"], cs[:, :, :])
            chain_dma(G, sO1, dbg["sloc"], cc_in.ap()[0:128, :])

    if PHASES >= 2:
        G.wait_ge(sC.h, 1)
        chain_dma(G, sQ, drow[0:1, :], cc_out.ap()[128::129, 0:64])
        drow_ld = sQ.n
        V.wait_ge(sQ.h, drow_ld)
        for r in range(8):
            wsl = wrow[0:1, 64 * r: 64 * (r + 1)]
            for m in range(8):
                dsl = drow[0:1, 64 * m: 64 * (m + 1)]
                fa = cmb[0:1, 8 * r + m: 8 * r + m + 1]
                fb = cmb[0:1, 64 + 8 * r + m: 64 + 8 * r + m + 1]
                if m == 0:
                    V.tensor_scalar(wsl, dsl, fa, fb, ALU.mult, ALU.add)
                    vedge()
                else:
                    V.tensor_scalar(small[0:1, 0:64], dsl, fa, fb, ALU.mult, ALU.add)
                    vedge()
                    V.tensor_mul(wsl, wsl, small[0:1, 0:64])
                    vedge()
        WROW_V = sV.n
        T.wait_ge(sV.h, WROW_V)
        T.matmul(pG[:, 0:512], onesf[:, :], wrow[0:1, :], start=True, stop=True)
        T.ops[-1][3].append((sT.h, 1))
        wb_done = sT.inc()
        V.wait_ge(sT.h, wb_done)
        V.tensor_copy(wbb[:, :, :], pG[:, 0:512].rearrange("p (a b) -> p a b", a=8)).then_inc(sV.h, 1)
        wbb_v = sV.inc()
        sgb = xdd2[:, 0, :]
        cscr = segscr[:, 3, :, :].rearrange("p a l -> p (a l)")
        for r in range(8):
            G.wait_ge(sV.h, sV.n)
            chain_dma(G, sQ, sgb, cc_out.ap()[129 * r: 129 * r + 128, :])
            sg_ld = sQ.n
            V.wait_ge(sQ.h, sg_ld)
            V.wait_ge(sV.h, wbb_v)
            for j in range(8):
                wbc = wbb[:, r, 8 * j:8 * j + 8].unsqueeze(2).broadcast_to([128, 8, 64])
                V.tensor_tensor(cscr.rearrange("p (a b) -> p a b", a=8),
                                sgb[:, 512 * j: 512 * (j + 1)].rearrange("p (a b) -> p a b", a=8),
                                wbc, ALU.mult)
                vedge()
                if r == 0:
                    V.tensor_copy(prev[:, 512 * j:512 * (j + 1)], cscr)
                    vedge()
                else:
                    V.tensor_add(prev[:, 512 * j:512 * (j + 1)],
                                 prev[:, 512 * j:512 * (j + 1)], cscr)
                    vedge()
        SIN_V = sV.n
        if DEBUG >= 2:
            G.wait_ge(sV.h, SIN_V)
            chain_dma(G, sO1, dbg["sin"], prev[:, :])
            V.wait_ge(sO1.h, sO1.n)

        eb_done = {}
        ypair_e = {}
        qcomb = {}
        for c in range(NCK):
            t0 = c * CHUNK
            csc = cs[:, c, :]
            V.wait_ge(sV.h, sV.n)
            V.wait_ge(sT.h, sT.n)
            V.tensor_copy(prev_bf[:, :], prev[:, :]).then_inc(sV.h, 1)
            pbf_v = sV.inc()
            G.wait_ge(sV.h, pbf_v)
            chain_dma(G, sQ, sgb, d_stsp[c])
            stld = sQ.n
            for qq in range(4):
                for hh in range(16):
                    h = 16 * qq + hh
                    hg = c * NH + h
                    q = h % 2
                    pr = h // 2
                    prs = pr % 8
                    pbc = pG[:, 256 * q: 256 * (q + 1)]
                    if hg >= 2:
                        T.wait_ge(sS.h, eb_done[hg - 2])
                    else:
                        T.wait_ge(sV.h, SIN_V)
                    T.wait_ge(sV.h, pbf_v)
                    oh = identf[0:64, h:h + 1].broadcast_to([64, 128])
                    T.matmul(pbc, oh, csc[:, :], start=True, stop=True)
                    T.ops[-1][3].append((sT.h, 1))
                    bc_done = sT.inc()
                    S.wait_ge(sT.h, bc_done)
                    qg = c * 4 + qq
                    if qg > 0 and hh == 0:
                        S.wait_ge(sV.h, qcomb[qg - 1])
                    S.activation(Eq[64 * q:64 * (q + 1), pr % 8, :], pbc[64 * q:64 * (q + 1), :],
                                 AF.Exp).then_inc(sS.h, 1)
                    eb_done[hg] = sS.inc()
                    pY = pBIG[:, 256 * prs: 256 * (prs + 1)]
                    if q == 0 and (prs in ypair_e):
                        T.wait_ge(sV.h, ypair_e[prs])
                    T.matmul(pY[64 * q:64 * (q + 1), :], prev_bf[:, 64 * h:64 * (h + 1)],
                             CT[:, t0:t0 + CHUNK], start=True, stop=True,
                             tile_position=(0, 64 * q))
                    T.ops[-1][3].append((sT.h, 1))
                    yo_done = sT.inc()
                    if q == 1 and pr % 2 == 1:
                        V.wait_ge(sT.h, yo_done)
                        pslot0 = (pr - 1) % 8
                        vsl = (pr - 1) % 8
                        V.tensor_copy(
                            yoffq[:, vsl:vsl + 2, :],
                            pBIG[:, 256 * pslot0: 256 * (pslot0 + 2)].rearrange("p (a l) -> p a l", a=2),
                        ).then_inc(sV.h, 1)
                        ypair_e[pslot0] = sV.inc()
                        ypair_e[(pslot0 + 1) % 8] = ypair_e[pslot0]
                V.wait_ge(sS.h, eb_done[c * NH + 16 * qq + 15])
                V.wait_ge(sV.h, sV.n)
                V.tensor_mul(yoffq[:, :, :].rearrange("p a l -> p (a l)"),
                             yoffq[:, :, :].rearrange("p a l -> p (a l)"),
                             Eq[:, :, :].rearrange("p a l -> p (a l)"))
                vedge()
                xsl = x_fm[:, 8 * qq:8 * (qq + 1), t0:t0 + CHUNK]
                V.tensor_add(xsl, xsl,
                             yoffq[:, :, :]).then_inc(sV.h, 1)
                qcomb[qg] = sV.inc()
            if c < NCK - 1:
                V.wait_ge(sQ.h, stld)
                for j in range(8):
                    cdbc = cdb[:, c, 8 * j:8 * j + 8].unsqueeze(2).broadcast_to([128, 8, 64])
                    prv = prev[:, j * 512:(j + 1) * 512].rearrange("p (a b) -> p a b", a=8)
                    V.tensor_tensor(prv, prv, cdbc, ALU.mult)
                    vedge()
                    V.tensor_add(prev[:, j * 512:(j + 1) * 512], prev[:, j * 512:(j + 1) * 512],
                                 sgb[:, 512 * j:512 * (j + 1)])
                    vedge()
        P2B_V = sV.n
        if DEBUG >= 2:
            G.wait_ge(sV.h, P2B_V)
            chain_dma(G, sO1, dbg["y"], x_fm[:, :, :])


    if PHASES >= 3:
        hard_barrier(0)
        if DEBUG:
            V.wait_ge(sO1.h, sO1.n)
        G.wait_ge(sV.h, P2B_V)
        chain_dma(G, sP, xnT[:, :, :], d_xnsp)
        xn_rld = sP.n
        if DEBUG >= 3:
            G.wait_ge(sP.h, xn_rld)
            chain_dma(G, sO1, dbg["xnT2"], xnT[:, :, :])
        T.wait_ge(sP.h, xn_rld)
        zmm_done = {}
        zevac_done = {}
        for mt in range(ZT):
            slot = mt % 2
            sWs = sW0 if slot == 0 else sW1
            if mt >= 2:
                G.wait_ge(sT.h, zmm_done[mt - 2])
            else:
                G.wait_ge(sV.h, P2B_V)
            G.dma_start(out=wring[:, slot, :, :], in_=d_wz[:, mt, :, :]).then_inc(sWs.h, 16)
            w_ld = sWs.inc(16)
            ps = pBIG[:, 1024 * slot: 1024 * (slot + 1)]
            T.wait_ge(sWs.h, w_ld)
            if mt >= 2:
                T.wait_ge(sS.h, zevac_done[mt - 2])
            for k in range(NKT):
                st, sp = (k == 0), (k == NKT - 1)
                T.matmul(ps[:, 0:512], wring[:, slot, k, :], xnT[:, k, HALO:HALO + 512], start=st, stop=sp)
                T.matmul(ps[:, 512:1024], wring[:, slot, k, :], xnT[:, k, HALO + 512:HALO + 1024], start=st, stop=sp)
            T.ops[-1][3].append((sT.h, 1))
            zmm_done[mt] = sT.inc()
            S.wait_ge(sT.h, zmm_done[mt])
            if mt >= 1:
                S.wait_ge(sV.h, sV.n)
            S.activation(zsb[:, :], ps[:, :], AF.Silu).then_inc(sS.h, 1)
            zevac_done[mt] = sS.inc()
            V.wait_ge(sS.h, zevac_done[mt])
            V.tensor_mul(x_fm[:, mt, :], x_fm[:, mt, :], zsb[:, :]).then_inc(sV.h, 1)
            g0_done = sV.inc()
            S.wait_ge(sV.h, g0_done)
            if mt >= 1:
                S.wait_ge(sT.h, sT.n)
            S.activation(sqsb[:, :], x_fm[:, mt, :], AF.Square).then_inc(sS.h, 1)
            sq_done = sS.inc()
            V.wait_ge(sS.h, sq_done)
            V.tensor_scalar(x_fm[:, mt, :], x_fm[:, mt, :], nw[:, mt:mt + 1], None,
                            ALU.mult).then_inc(sV.h, 1)
            sV.inc()
            T.wait_ge(sS.h, sq_done)
            T.matmul(pE[0:1, :], onescol[:, 0:1], sqsb[:, 0:512], start=(mt == 0), stop=(mt == ZT - 1))
            T.matmul(pF[0:1, :], onescol[:, 0:1], sqsb[:, 512:1024], start=(mt == 0), stop=(mt == ZT - 1))
            T.ops[-1][3].append((sT.h, 1))
            sT.inc()
        ZP_T = sT.n
        ZP_V = sV.n
        if DEBUG >= 3:
            G.wait_ge(sS.h, sS.n)
            G.wait_ge(sV.h, ZP_V)
            chain_dma(G, sO1, dbg["zlast"], zsb[:, :])
            chain_dma(G, sO1, dbg["sqlast"], sqsb[:, :])
            V.wait_ge(sO1.h, sO1.n)
        V.wait_ge(sT.h, ZP_T)
        V.tensor_copy(csrow[0:1, 0:512], pE[0:1, :])
        V.tensor_copy(csrow[0:1, 512:1024], pF[0:1, :])
        vedge()
        V.tensor_scalar(csrow[0:1, :], csrow[0:1, :], 1.0 / DI, EPS, ALU.mult, ALU.add)
        vedge()
        S.wait_ge(sV.h, sV.n)
        S.activation(csrow[0:1, :], csrow[0:1, :], AF.Ln).then_inc(sS.h, 1)
        S.wait_ge(sS.h, sS.inc())
        S.activation(csrow[0:1, :], csrow[0:1, :], AF.Exp, scale=-0.5).then_inc(sS.h, 1)
        inv_done = sS.inc()
        T.wait_ge(sS.h, inv_done)
        for tt in range(8):
            T.transpose(pG[:, 8 * tt: 8 * tt + 8][:, 0:1], csrow[0:1, 128 * tt:128 * (tt + 1)],
                        identf[0:1, 0:1])
        T.ops[-1][3].append((sT.h, 1))
        invt_done = sT.inc()
        V.wait_ge(sT.h, invt_done)
        V.tensor_copy(small[:, 0:8], pG[:, 0:64].rearrange("p (a b) -> p a b", a=8)[:, :, 0]).then_inc(sV.h, 1)
        invcol_v = sV.inc()
        hard_barrier(1)
        T.wait_ge(sV.h, ZP_V)
        T.wait_ge(sV.h, invcol_v)
        rs_spill = []
        opmm = {}
        optr = {}
        stg_v = {}
        for dt_ in range(16):
            slot = dt_ % 2
            sWs = sW2 if slot == 0 else sW3
            if dt_ < 2:
                G.wait_ge(sT.h, zmm_done[ZT - 1])
            else:
                G.wait_ge(sT.h, opmm[dt_ - 2])
            G.dma_start(out=wdoB[:, slot, :, :],
                        in_=d_wo[:, :, 128 * dt_: 128 * (dt_ + 1)]).then_inc(sWs.h, 16)
            wo_ld = sWs.inc(16)
            ps = pBIG[:, 1024 * slot: 1024 * (slot + 1)]
            T.wait_ge(sWs.h, wo_ld)
            if dt_ >= 2:
                T.wait_ge(sV.h, stg_v[dt_ - 2])
            for k in range(ZT):
                st, sp = (k == 0), (k == ZT - 1)
                T.matmul(ps[:, 0:512], wdoB[:, slot, k, :], x_fm[:, k, 0:512], start=st, stop=sp)
                T.matmul(ps[:, 512:1024], wdoB[:, slot, k, :], x_fm[:, k, 512:1024], start=st, stop=sp)
            T.ops[-1][3].append((sT.h, 1))
            opmm[dt_] = sT.inc()
            V.wait_ge(sT.h, opmm[dt_])
            if dt_ >= 1:
                V.wait_ge(sT.h, optr[dt_ - 1])
            V.tensor_copy(sqsb[:, :], ps[:, :]).then_inc(sV.h, 1)
            ev = sV.inc()
            T.wait_ge(sV.h, ev)
            if dt_ >= 1:
                T.wait_ge(sV.h, stg_v[dt_ - 1])
            for tt in range(8):
                inst = T.transpose(pT[:, 128 * tt: 128 * (tt + 1)],
                                   sqsb[:, 128 * tt: 128 * (tt + 1)], identb[:, :])
            inst.then_inc(sT.h, 1)
            optr[dt_] = sT.inc()
            V.wait_ge(sT.h, optr[dt_])
            if dt_ >= 1:
                V.wait_ge(sP.h, rs_spill[-1])
            for tt in range(8):
                V.tensor_scalar(zsb[:, 128 * tt: 128 * (tt + 1)], pT[:, 128 * tt: 128 * (tt + 1)],
                                small[:, tt:tt + 1], None, ALU.mult)
            V.ops[-1][3].append((sV.h, 1))
            stg_v[dt_] = sV.inc()
            G.wait_ge(sV.h, stg_v[dt_])
            chain_dma(G, sP, d_rssp[:, :, 128 * dt_: 128 * (dt_ + 1)].rearrange("t p c -> p t c"),
                      zsb[:, :].rearrange("p (t c) -> p t c", t=8))
            rs_spill.append(sP.n)
        P3_V = sV.n
        if DEBUG >= 3:
            G.wait_ge(sV.h, sV.n)
            chain_dma(G, sO1, dbg["ygated"], x_fm[:, :, :])
            chain_dma(G, sO1, dbg["inv"], small[:, 0:8])

    if PHASES >= 4:
        hard_barrier(2)
        T_P3 = sT.n
        if DEBUG:
            G.wait_ge(sO1.h, sO1.n)
        G.wait_ge(sP.h, rs_spill[-1])
        for tt in range(8):
            G.wait_ge(sT.h, T_P3)
            chain_dma(G, sQ, resid[:, tt, :], d_rssp[tt])
            sUp = sU0 if tt % 2 == 0 else sU1
            if tt >= 2:
                SY.wait_ge(sV.h, sV.n)
            else:
                SY.wait_ge(sV.h, stg_v[15])
            SY.dma_start(out=(ubuf4 if tt % 2 == 0 else lnsq2)[:, :],
                         in_=d_u[HALO + 128 * tt: HALO + 128 * (tt + 1), :]).then_inc(sUp.h, 16)
            u_ld = sUp.inc(16)
            V.wait_ge(sQ.h, sQ.n)
            V.wait_ge(sUp.h, u_ld)
            V.tensor_add(resid[:, tt, :], resid[:, tt, :],
                         (ubuf4 if tt % 2 == 0 else lnsq2)[:, :]).then_inc(sV.h, 1)
            sV.inc()
        rs_rld = sQ.n
        RSADD_V = sV.n
        if DEBUG >= 3:
            G.wait_ge(sQ.h, rs_rld)
            for _tt in range(8):
                chain_dma(G, sO1, dbg["resid"][_tt], resid[:, _tt, :])
            V.wait_ge(sO1.h, sO1.n)
        SY.wait_ge(sQ.h, rs_rld)
        chain_dma(SY, sU, dscol[:, :], d_ds)
        ds_ld = sU.n
        xn2bf = lnsq2.bitcast(BF16)
        for tt in range(8):
            V.wait_ge(sV.h, RSADD_V)
            V.memset(stat[:, 1:2], 0.0).then_inc(sV.h, 1)
            sqz = sV.inc()
            V.reduce_sum(stat[:, 0:1], resid[:, tt, :], axis=AX.X).then_inc(sV.h, 1)
            sum_done = sV.inc()
            S.wait_ge(sV.h, sqz)
            if tt >= 1:
                S.wait_ge(sT.h, sT.n)
            S.activation(pscr[:, :], resid[:, tt, :], AF.Square,
                         accum_out=stat[:, 1:2]).then_inc(sS.h, 1)
            sq_done = sS.inc()
            V.wait_ge(sS.h, sq_done)
            V.tensor_scalar(stat[:, 2:3], stat[:, 0:1], 1.0 / D, None, ALU.mult)
            V.tensor_scalar(stat[:, 3:4], stat[:, 1:2], 1.0 / D, EPS, ALU.mult, ALU.add).then_inc(sV.h, 1)
            V.wait_ge(sV.h, sV.inc())
            V.tensor_scalar(stat[:, 4:5], stat[:, 2:3], stat[:, 2:3], None, ALU.mult).then_inc(sV.h, 1)
            V.wait_ge(sV.h, sV.inc())
            V.tensor_sub(stat[:, 5:6], stat[:, 3:4], stat[:, 4:5]).then_inc(sV.h, 1)
            var_done = sV.inc()
            S.wait_ge(sV.h, var_done)
            S.activation(stat[:, 6:7], stat[:, 5:6], AF.Ln).then_inc(sS.h, 1)
            S.wait_ge(sS.h, sS.inc())
            S.activation(stat[:, 7:8], stat[:, 6:7], AF.Exp, scale=-0.5).then_inc(sS.h, 1)
            rstd_done = sS.inc()
            V.wait_ge(sS.h, rstd_done)
            V.tensor_scalar(stat[:, 8:9], stat[:, 2:3], stat[:, 7:8], -1.0,
                            ALU.mult, ALU.mult).then_inc(sV.h, 1)
            nb_done = sV.inc()
            V.wait_ge(sV.h, nb_done)
            if tt >= 1:
                V.wait_ge(sT.h, sT.n)
            V.tensor_scalar(xn2bf[:, 0:2048], resid[:, tt, :], stat[:, 7:8],
                            stat[:, 8:9], ALU.mult, ALU.add).then_inc(sV.h, 1)
            xn_done = sV.inc()
            for half in range(2):
                T.wait_ge(sV.h, xn_done)
                T.wait_ge(sV.h, sV.n)
                for g8 in range(8):
                    kt = half * 8 + g8
                    inst = T.transpose(pT[:, g8 * 128:(g8 + 1) * 128],
                                       xn2bf[:, kt * 128:(kt + 1) * 128], identb[:, :])
                inst.then_inc(sT.h, 1)
                tr_done = sT.inc()
                V.wait_ge(sT.h, tr_done)
                V.tensor_copy(xn2T[:, half * 8:(half + 1) * 8, 128 * tt: 128 * (tt + 1)],
                              pT[:, :].rearrange("p (g r) -> p g r", g=8)).then_inc(sV.h, 1)
                sV.inc()
        LN2_V = sV.n
        T.wait_ge(sV.h, LN2_V)
        for k in range(NKT):
            T.matmul(pE[0:1, :], onescol[:, 0:1], xn2T[:, k, 0:512], start=(k == 0), stop=(k == NKT - 1))
            T.matmul(pF[0:1, :], onescol[:, 0:1], xn2T[:, k, 512:1024], start=(k == 0), stop=(k == NKT - 1))
        T.ops[-1][3].append((sT.h, 1))
        cols_done = sT.inc()
        V.wait_ge(sT.h, cols_done)
        V.tensor_copy(colbf[0:1, 0:512], pE[0:1, :])
        V.tensor_copy(colbf[0:1, 512:1024], pF[0:1, :]).then_inc(sV.h, 1)
        colbf_v = sV.inc()

        hard_barrier(3)
        NR = 4
        IPR = 16
        hv_done = {}
        for rr in range(NR):
            for ii in range(IPR):
                it = rr * IPR + ii
                G.wait_ge(sV.h, colbf_v)
                if it >= 1:
                    G.wait_ge(sT.h, gmm_prev)
                G.dma_start(out=wgu[:, 0, :, :], in_=d_wg[:, it, :, :]).then_inc(sW0.h, 16)
                g_ld = sW0.inc(16)
                if it >= 1:
                    G.wait_ge(sT.h, umm_prev)
                G.dma_start(out=wgu[:, 1, :, :], in_=d_wu[:, it, :, :]).then_inc(sW1.h, 16)
                u_ld = sW1.inc(16)
                T.wait_ge(sW0.h, g_ld)
                T.wait_ge(sV.h, colbf_v)
                if it >= 1:
                    T.wait_ge(sS.h, sg_prev)
                for k in range(NKT):
                    st, sp = (k == 0), (k == NKT - 1)
                    T.matmul(pBIG[:, 0:512], wgu[:, 0, k, :], xn2T[:, k, 0:512], start=st, stop=sp)
                    T.matmul(pBIG[:, 512:1024], wgu[:, 0, k, :], xn2T[:, k, 512:1024], start=st, stop=sp)
                T.matmul(pBIG[:, 0:512], neg128[0:1, :], colbf[0:1, 0:512], start=False, stop=True)
                T.matmul(pBIG[:, 512:1024], neg128[0:1, :], colbf[0:1, 512:1024], start=False, stop=True)
                T.ops[-1][3].append((sT.h, 1))
                gmm = gmm_prev = sT.inc()
                T.wait_ge(sW1.h, u_ld)
                if it >= 1:
                    T.wait_ge(sV.h, hv_done[it - 1])
                for k in range(NKT):
                    st, sp = (k == 0), (k == NKT - 1)
                    T.matmul(pBIG[:, 1024:1536], wgu[:, 1, k, :], xn2T[:, k, 0:512], start=st, stop=sp)
                    T.matmul(pBIG[:, 1536:2048], wgu[:, 1, k, :], xn2T[:, k, 512:1024], start=st, stop=sp)
                T.matmul(pBIG[:, 1024:1536], neg128[0:1, :], colbf[0:1, 0:512], start=False, stop=True)
                T.matmul(pBIG[:, 1536:2048], neg128[0:1, :], colbf[0:1, 512:1024], start=False, stop=True)
                T.ops[-1][3].append((sT.h, 1))
                umm = umm_prev = sT.inc()
                S.wait_ge(sT.h, gmm)
                if it >= 1:
                    S.wait_ge(sV.h, hv_done[it - 1])
                S.activation(sgscr[:, :], pBIG[:, 0:1024], AF.Silu,
                             scale=gsc[:, it:it + 1]).then_inc(sS.h, 1)
                sg_done = sg_prev = sS.inc()
                V.wait_ge(sT.h, umm)
                V.wait_ge(sS.h, sg_done)
                V.scalar_tensor_tensor(hsb[:, ii, :], pBIG[:, 1024:2048], usc[:, it:it + 1],
                                       sgscr[:, :], ALU.mult, ALU.mult).then_inc(sV.h, 1)
                hv_done[it] = sV.inc()
            hard_barrier(4)
            T.wait_ge(sV.h, hv_done[rr * IPR + IPR - 1])
            for ii in range(IPR):
                T.matmul(pG[0:1, :], onescol[:, 0:1], hsb[:, ii, 0:512],
                         start=(ii == 0), stop=(ii == IPR - 1))
            T.ops[-1][3].append((sT.h, 1))
            hsA = sT.inc()
            V.wait_ge(sT.h, hsA)
            V.tensor_copy(hsbf[0:1, 0:512], pG[0:1, :]).then_inc(sV.h, 1)
            hsA_v = sV.inc()
            T.wait_ge(sV.h, hsA_v)
            for ii in range(IPR):
                T.matmul(pG[0:1, :], onescol[:, 0:1], hsb[:, ii, 512:1024],
                         start=(ii == 0), stop=(ii == IPR - 1))
            T.ops[-1][3].append((sT.h, 1))
            hsB = sT.inc()
            V.wait_ge(sT.h, hsB)
            V.tensor_copy(hsbf[0:1, 512:1024], pG[0:1, :]).then_inc(sV.h, 1)
            hs_v = sV.inc()
            for ht in range(16):
                ks = ht % 2
                sWk = sW2 if ks == 0 else sW3
                if rr + ht > 0:
                    G.wait_ge(sT.h, dmm_prev)
                G.dma_start(out=wdd[:, ks, :].rearrange("p (a b) -> p a b", a=16),
                            in_=d_wd[:, rr * IPR:(rr + 1) * IPR, 128 * ht: 128 * (ht + 1)]).then_inc(sWk.h, 16)
                wd_ld = sWk.inc(16)
                T.wait_ge(sWk.h, wd_ld)
                T.wait_ge(sV.h, hs_v)
                if rr + ht > 0:
                    T.wait_ge(sV.h, dse_prev)
                wdt = wdd[:, ks, :].rearrange("p (a b) -> p a b", a=16)
                for ki in range(IPR):
                    st = (ki == 0)
                    T.matmul(pE[:, :], wdt[:, ki, :], hsb[:, ki, 0:512], start=st, stop=False)
                    T.matmul(pF[:, :], wdt[:, ki, :], hsb[:, ki, 512:1024], start=st, stop=False)
                T.matmul(pE[:, :], neg128[0:1, :], hsbf[0:1, 0:512], start=False, stop=True)
                T.matmul(pF[:, :], neg128[0:1, :], hsbf[0:1, 512:1024], start=False, stop=True)
                T.ops[-1][3].append((sT.h, 1))
                dmm = dmm_prev = sT.inc()
                V.wait_ge(sT.h, dmm)
                V.wait_ge(sU.h, ds_ld)
                V.tensor_scalar(dwn[:, 0:512], pE[:, :], dscol[:, ht:ht + 1], None, ALU.mult)
                V.tensor_scalar(dwn[:, 512:1024], pF[:, :], dscol[:, ht:ht + 1], None, ALU.mult).then_inc(sV.h, 1)
                dse = sV.inc()
                T.wait_ge(sV.h, dse)
                if rr + ht > 0:
                    T.wait_ge(sV.h, radd_prev)
                for tt in range(8):
                    inst = T.transpose(pT[:, 128 * tt: 128 * (tt + 1)],
                                       dwn[:, 128 * tt: 128 * (tt + 1)], identb[:, :])
                inst.then_inc(sT.h, 1)
                dtr = sT.inc()
                V.wait_ge(sT.h, dtr)
                V.tensor_add(resid[:, :, 128 * ht: 128 * (ht + 1)],
                             resid[:, :, 128 * ht: 128 * (ht + 1)],
                             pT[:, :].rearrange("p (a b) -> p a b", a=8)).then_inc(sV.h, 1)
                dse_prev = radd_prev = sV.inc()
        P4_V = sV.n

        if DEBUG >= 4:
            G.wait_ge(sV.h, LN2_V)
            chain_dma(G, sO1, dbg["xn2T"], xn2T[:, :, :])
            G.wait_ge(sV.h, hv_done[0])
            chain_dma(G, sO1, dbg["h0"], hsb[:, 0, :])
        hard_barrier(5)
        G.wait_ge(sV.h, P4_V)
        for tt in range(8):
            chain_dma(G, sO, d_out[tt * 128:(tt + 1) * 128, :], resid[:, tt, :])
        G.wait_ge(sO.h, sO.n)
        if DEBUG:
            G.wait_ge(sO1.h, sO1.n)

    if PHASES < 4:
        V.wait_ge(sS.h, sS.n)
        V.wait_ge(sV.h, sV.n)
        V.memset(lnscr[:, 0, :], 0.0).then_inc(sV.h, 1)
        fin = sV.inc()
        G.wait_ge(sV.h, fin)
        for tt in range(8):
            chain_dma(G, sO, d_out[tt * 128:(tt + 1) * 128, :], lnscr[:, 0, :])
        G.wait_ge(sO.h, sO.n)
        if DEBUG:
            G.wait_ge(sO1.h, sO1.n)

    with nc.Block() as block:
        @block.sync
        def _(e):
            SY.replay(e)

        @block.tensor
        def _(e):
            T.replay(e)

        @block.vector
        def _(e):
            V.replay(e)

        @block.scalar
        def _(e):
            S.replay(e)

        @block.gpsimd
        def _(e):
            G.replay(e)

    for g in reversed(guards):
        g.__exit__(None, None, None)
    return nc



def _prep_shared(inputs):
    W = np.ascontiguousarray(np.asarray(inputs["L0_attn_in_proj_w"], dtype=np.float32))
    WzT = np.ascontiguousarray(W[:DI].T)
    WxT = np.ascontiguousarray(W[DI:].T)
    WxTp = np.zeros((D, 35 * 128), np.float32)
    WxTp[:, :4416] = WxT
    wx = np.ascontiguousarray(WxTp.reshape(NKT, 128, 35, 128).transpose(1, 2, 0, 3))
    wz = np.ascontiguousarray(WzT.reshape(NKT, 128, ZT, 128).transpose(1, 2, 0, 3))
    Wo = np.asarray(inputs["L0_attn_out_proj_w"], dtype=np.float32)
    wo = np.ascontiguousarray(Wo.T.reshape(ZT, 128, D).transpose(1, 0, 2))
    Wg = np.asarray(inputs["L0_ffn_gate_w"], dtype=np.int32)
    Wu = np.asarray(inputs["L0_ffn_up_w"], dtype=np.int32)
    Wd = np.asarray(inputs["L0_ffn_down_w"], dtype=np.int32)
    wg = np.ascontiguousarray(Wg.T.reshape(NKT, 128, IT, 128).transpose(1, 2, 0, 3))
    wu = np.ascontiguousarray(Wu.T.reshape(NKT, 128, IT, 128).transpose(1, 2, 0, 3))
    wd = np.ascontiguousarray(Wd.T.reshape(IT, 128, D).transpose(1, 0, 2))
    cw = np.ascontiguousarray(np.asarray(inputs["L0_attn_conv_w"], np.float32).reshape(XT, 128, 4).transpose(1, 0, 2))
    cb = np.ascontiguousarray(np.asarray(inputs["L0_attn_conv_b"], np.float32).reshape(XT, 128).T)
    dtbv = np.ascontiguousarray(np.asarray(inputs["L0_attn_dt_bias"], np.float32).reshape(64, 1))
    alogv = np.ascontiguousarray(np.asarray(inputs["L0_attn_A_log"], np.float32).reshape(64, 1))
    Dsk = np.asarray(inputs["L0_attn_D_skip"], np.float32)
    ddiag = np.zeros((128, ZT, 128), np.float32)
    for t in range(ZT):
        for k in range(128):
            ddiag[k, t, k] = Dsk[2 * t + (k >= 64)]
    nwv = np.ascontiguousarray(np.asarray(inputs["L0_attn_norm_w"], np.float32).reshape(ZT, 128).T)
    gs = np.ascontiguousarray(np.asarray(inputs["L0_ffn_gate_scale"], np.float32).reshape(IT, 128).T)
    us = np.ascontiguousarray(np.asarray(inputs["L0_ffn_up_scale"], np.float32).reshape(IT, 128).T)
    ds = np.ascontiguousarray(np.asarray(inputs["L0_ffn_down_scale"], np.float32).reshape(16, 128).T)
    identfv = np.eye(128, dtype=np.float32)
    tri = np.zeros((128, 2, CHUNK), np.float32)
    for half in range(2):
        for p in range(128):
            tri[p, half, half * 128 + p:] = 1.0
    return dict(wx=wx, wz=wz, wo=wo, wg=wg, wu=wu, wd=wd, convw=cw, convb=cb,
                dtb=dtbv, alog=alogv, ddiag=ddiag, nw=nwv, gscale=gs, uscale=us,
                dscale=ds, identf=identfv, tri=tri)


def _prep_core(inputs, core):
    b, q = core // 4, core % 4
    u = np.asarray(inputs["L0_attn_u"], np.float32)[b]
    t0 = q * LOCAL
    uh = np.zeros((TH, D), np.float32)
    if t0 >= HALO:
        uh[:HALO] = u[t0 - HALO:t0]
    uh[HALO:] = u[t0:t0 + LOCAL]
    Tm = np.zeros((8, 8), np.float32)
    Mm = np.zeros(8, np.float32)
    for r in range(8):
        rb, rq = r // 4, r % 4
        if rb == b and rq < q:
            Mm[r] = 1.0
            for m in range(8):
                mb, mq = m // 4, m % 4
                if mb == b and rq < mq < q:
                    Tm[r, m] = 1.0
    fA = Tm.copy()
    fB = 1.0 - Tm
    fA[:, 0] *= Mm
    fB[:, 0] *= Mm
    cmb = np.zeros((1, 128), np.float32)
    cmb[0, 0:64] = fA.reshape(-1)
    cmb[0, 64:128] = fB.reshape(-1)
    return dict(u=np.ascontiguousarray(uh), cmb=cmb)


_CACHE = {}


def kernel(**inputs):
    shared = _prep_shared(inputs)
    if "nc" not in _CACHE:
        _CACHE["nc"] = build_program()
    nc = _CACHE["nc"]
    in_maps = []
    for c in range(8):
        m = dict(shared)
        m.update(_prep_core(inputs, c))
        in_maps.append(m)
    res = run_bass_kernel_spmd(nc, in_maps, core_ids=list(range(8)))
    outs = [res.results[c]["out"] for c in range(8)]
    full = np.stack([np.concatenate(outs[0:4], 0), np.concatenate(outs[4:8], 0)])
    return full.astype(np.float32)
